# version 15
# baseline (speedup 1.0000x reference)
import math
import numpy as np

import concourse.bass as bass
import concourse.mybir as mybir
from concourse.bass_utils import run_bass_kernel_spmd

F32 = mybir.dt.float32
BF16 = mybir.dt.bfloat16
AF = mybir.ActivationFunctionType
ALU = mybir.AluOpType
AX = mybir.AxisListType

S = 4096
D = 2048
HD = 128
INV_SQRT_HD = 1.0 / math.sqrt(HD)
LN_EPS = 1e-5

ENGS = ("sync", "gpsimd", "vector", "scalar", "tensor")
PART_BYTES = 229344


class Sched:

    def __init__(self):
        self.ops = {e: [] for e in ENGS}
        self.cnt = {}
        self.waited = {e: {} for e in ENGS}

    def wait(self, eng, tok):
        if tok is None:
            return
        sname, v = tok
        if v <= 0 or self.waited[eng].get(sname, 0) >= v:
            return
        self.waited[eng][sname] = v
        self.ops[eng].append(("w", sname, v))

    def op(self, eng, fn, inc=None, n=1):
        tok = None
        if inc is not None:
            self.cnt[inc] = self.cnt.get(inc, 0) + n
            tok = (inc, self.cnt[inc])
        self.ops[eng].append(("o", fn, inc, n))
        return tok

    def cur(self, sem):
        return (sem, self.cnt.get(sem, 0))


def build_program():
    nc = bass.Bass("TRN2", target_bir_lowering=False, debug=False)

    xq = nc.dram_tensor("xq", [32, 128, 2048], F32, kind="ExternalInput").ap()
    wq = nc.dram_tensor("wq", [12, 128, 2048], F32, kind="ExternalInput").ap()
    wo = nc.dram_tensor("wo", [4, 128, 2048], F32, kind="ExternalInput").ap()
    xres = nc.dram_tensor("xres", [16, 128, 1024], F32, kind="ExternalInput").ap()
    up = nc.dram_tensor("up", [64, 128, 2048], F32, kind="ExternalInput").ap()
    down = nc.dram_tensor("down", [8, 8, 128, 2048], F32,
                          kind="ExternalInput").ap()
    smalls = nc.dram_tensor("smalls", [128, 112], F32, kind="ExternalInput").ap()
    bv = nc.dram_tensor("bv", [1, 512], F32, kind="ExternalInput").ap()
    y = nc.dram_tensor("y", [16, 128, 1024], F32, kind="ExternalOutput").ap()

    cbuf2 = [nc.dram_tensor("cbufa", [8192, 512], BF16),
             nc.dram_tensor("cbufb", [8192, 512], BF16)]
    rbuf2 = [nc.dram_tensor("rbufa", [2048, 512], BF16),
             nc.dram_tensor("rbufb", [2048, 512], BF16)]
    xln1d = nc.dram_tensor("xln1d", [16, 128, 1024], F32)
    recd = nc.dram_tensor("recd", [2, 512], F32)

    def sb(name, shape, dtype, off):
        per = int(np.prod(shape[1:])) * mybir.dt.size(dtype)
        assert off + per <= PART_BYTES, f"{name} overflows: {off}+{per}"
        return nc.alloc_sbuf_tensor_at(name, shape, dtype, offset=off)

    ones_bf = sb("ones_bf", [128, 128], BF16, 16512)
    ones_f = sb("ones_f", [1, 128], F32, 16768)
    masks = sb("masks", [128, 4, 512], BF16, 17280)
    sm = sb("sm", [128, 112], F32, 21376)
    dsos = sb("dsos", [128, 16], F32, 21824)
    bv_sb = sb("bv_sb", [1, 512], F32, 21888)
    bv_bf = sb("bv_bf", [1, 512], BF16, 23936)
    negU = sb("negU", [128, 1], F32, 24960)
    uw = sb("uw", [1, 16], F32, 24992)
    nub = sb("nub", [1, 1], BF16, 25056)
    rec_f = sb("rec_f", [1, 512], F32, 25088)
    rec_f1 = sb("rec_f1", [1, 512], F32, 21888)
    recs = [rec_f, rec_f1]
    recbc = sb("recbc", [128, 2, 512], BF16, 27136)
    qsq = sb("qsq", [128, 512], BF16, 29184)
    Q_sb = sb("Q_sb", [128, 4, 4096], BF16, 30208)
    K_sb = sb("K_sb", [128, 4, 4096], BF16, 62976)
    V_sb = sb("V_sb", [128, 32, 512], BF16, 95744)
    wq_bf = sb("wq_bf", [128, 16, 1536], BF16, 128512)
    xstage = sb("xstage", [128, 2, 2048], F32, 177664)
    xblk = sb("xblk", [128, 16, 1024], BF16, 194048)
    O_sb = sb("O_sb", [128, 4, 4096], BF16, 128512)
    PT = sb("PT", [128, 2, 1024], BF16, 161280)
    cstage = sb("cstage", [128, 4, 512], BF16, 165376)
    wout_bf = sb("wout_bf", [128, 4, 2048], BF16, 194048)
    bigf = sb("bigf", [128, 16, 1024], F32, 30208)
    xln = sb("xln", [128, 16, 1024], BF16, 95744)
    ustage = sb("ustage", [128, 2, 2048], F32, 128512)
    ub16 = sb("ub16", [128, 2, 2048], BF16, 144896)
    rows = sb("rows", [1, 4096], F32, 212480)
    rstage = sb("rstage", [128, 2, 1024], BF16, 153088)
    xrstage = sb("xrstage", [128, 2, 512], F32, 157184)
    rb16 = sb("rb16", [128, 2, 1024], BF16, 169472)
    sqb = sb("sqb", [128, 1024], BF16, 173568)
    t1 = sb("t1", [128, 1024], F32, 175616)
    meanbc = sb("meanbc", [128, 1024], F32, 179712)
    rstdbc = sb("rstdbc", [128, 1024], F32, 183808)
    xplstage = sb("xplstage", [128, 2, 1024], F32, 187904)
    dstage = sb("dstage", [128, 2, 2048], F32, 196096)
    dn16 = sb("dn16", [128, 8, 16, 128], BF16, 153088)
    hgrp = sb("hgrp", [128, 8, 1024], BF16, 212480)
    xpl2 = sb("xpl2", [128, 2, 1024], F32, 153088)
    rb16b = sb("rb16b", [128, 2, 1024], BF16, 161280)
    sqb2 = sb("sqb2", [128, 1024], BF16, 165376)
    t1b = sb("t1b", [128, 1024], F32, 167424)
    meanbc2 = sb("meanbc2", [128, 1024], F32, 171520)
    rstdbc2 = sb("rstdbc2", [128, 1024], F32, 175616)
    fstage = sb("fstage", [128, 2, 1024], F32, 179712)
    psA = nc.alloc_psum_tensor("psA", [128, 1024], F32)
    psB = nc.alloc_psum_tensor("psB", [128, 1024], F32)
    psO = nc.alloc_psum_tensor("psO", [128, 512], F32)
    psM = nc.alloc_psum_tensor("psM", [128, 512], F32)
    psC = nc.alloc_psum_tensor("psC", [128, 512], F32)
    psU = nc.alloc_psum_tensor("psU", [128, 512], F32)

    sems = {n: nc.alloc_semaphore(n) for n in
            ("ld", "st", "pe", "act", "dve", "pool", "cc",
             "x0", "x1", "r0", "r1", "xr0", "xr1", "us0", "us1",
             "dd0", "dd1", "p0", "p1", "c0", "c1", "xo0", "xo1",
             "yo0", "yo1", "rb0", "rb1", "c2", "c3")}

    s = Sched()
    wtok = {}
    rtok = {}

    def ps_half(which, half):
        t = psA if which == "A" else psB
        return t[:, half * 512:(half + 1) * 512]

    def mm(out_ap, lhsT, rhs, slot, start, stop, extra_waits=(),
           inc_anyway=False):
        if start:
            s.wait("tensor", rtok.get(slot))
        for t in extra_waits:
            s.wait("tensor", t)
        do_inc = stop or inc_anyway
        tok = s.op("tensor",
                   lambda E, o=out_ap, l=lhsT, r=rhs, st_=start, sp=stop:
                   E.matmul(o, l, r, start=st_, stop=sp),
                   inc="pe" if do_inc else None)
        if stop:
            wtok[slot] = tok
        return tok

    def read_slot(eng, slot, fn, inc, extra_waits=()):
        s.wait(eng, wtok.get(slot))
        for t in extra_waits:
            s.wait(eng, t)
        tok = s.op(eng, fn, inc=inc)
        rtok[slot] = tok
        return tok

    def dma_in(out_ap, in_ap, sem="ld", extra_waits=()):
        for t in extra_waits:
            s.wait("sync", t)
        return s.op("sync", lambda E, o=out_ap, i=in_ap: E.dma_start(out=o, in_=i),
                    inc=sem, n=16)

    def dma_out(out_ap, in_ap, sem="st", extra_waits=()):
        for t in extra_waits:
            s.wait("gpsimd", t)
        return s.op("gpsimd", lambda E, o=out_ap, i=in_ap: E.dma_start(out=o, in_=i),
                    inc=sem, n=16)

    dma_in(sm[:, :], smalls)
    dma_in(bv_sb[:, :], bv)
    t_sm = t_bv = ("ld", 32)

    s.op("gpsimd", lambda E: E.memset(ones_bf[:, :], 1.0), inc="pool")
    s.op("gpsimd", lambda E: E.memset(ones_f[:, :], 1.0), inc="pool")
    for v in range(4):
        s.op("gpsimd", lambda E, v=v: E.memset(masks[:, v, :], 1.0), inc="pool")
        s.wait("gpsimd", s.cur("pool"))
        s.op("gpsimd", lambda E, v=v: E.affine_select(
            out=masks[:, v, :], in_=masks[:, v, :], pattern=[[1, 512]],
            compare_op=ALU.is_ge, fill=0.0, base=-128 * v, channel_multiplier=-1),
            inc="pool")
    s.op("gpsimd", lambda E: E.memset(uw[0:1, 6:7], -0.5 * math.log(HD)),
         inc="pool")
    s.op("gpsimd", lambda E: E.memset(uw[0:1, 7:8], LN_EPS), inc="pool")
    t_const = s.cur("pool")
    s.wait("tensor", t_const)
    s.wait("vector", t_const)
    s.wait("scalar", t_const)

    s.wait("vector", t_bv)
    t_bvbf = s.op("vector", lambda E: E.tensor_copy(bv_bf[:, :], bv_sb[:, :]),
                  inc="dve")
    s.wait("vector", t_sm)
    t_dsos = s.op("vector", lambda E: E.tensor_scalar(
        dsos[:, :], sm[:, 88:104], sm[:, 104:105], None, ALU.mult), inc="dve")

    xstage_tok = [None, None]
    xblk_pe_tok = None

    wq_tok = [None] * 12

    def load_wq_piece(i):
        tl = dma_in(xstage[:, i % 2, :], wq[i], sem=f"x{i % 2}",
                    extra_waits=[xstage_tok[i % 2]])
        s.wait("vector", tl)
        tok = s.op("vector",
                   lambda E, i=i: E.tensor_copy(
                       wq_bf[:, :, i * 128:(i + 1) * 128],
                       xstage[:, i % 2, :].rearrange("p (dc f) -> p dc f", dc=16)),
                   inc="dve")
        xstage_tok[i % 2] = tok
        wq_tok[i] = tok

    load_wq_piece(0)

    qk_slots = [[("A", 0), ("A", 1)], [("B", 0), ("B", 1)]]
    nslot = 0
    vslot = 0
    for tb in range(4):
        for sl in range(8):
            gslab = tb * 8 + sl
            tl = dma_in(xstage[:, sl % 2, :], xq[gslab], sem=f"x{sl % 2}",
                        extra_waits=[xstage_tok[sl % 2]])
            s.wait("vector", tl)
            if sl == 0:
                s.wait("vector", xblk_pe_tok)
            tok = s.op("vector",
                       lambda E, sl=sl: E.tensor_copy(
                           xblk[:, :, sl * 128:(sl + 1) * 128],
                           xstage[:, sl % 2, :].rearrange(
                               "p (dc f) -> p dc f", dc=16)),
                       inc="dve")
            xstage_tok[sl % 2] = tok
            if tb == 0:
                hi = 12 if sl == 7 else 2 + 2 * sl
                for i in range(1, hi):
                    if i < 12 and wq_tok[i] is None:
                        load_wq_piece(i)
        t_xblk = s.cur("dve")

        for et in range(8):
            pair = qk_slots[nslot % 2]
            nslot += 1
            for tc in range(2):
                for dc in range(16):
                    mm(ps_half(*pair[tc]),
                       wq_bf[:, dc, et * 128:(et + 1) * 128],
                       xblk[:, dc, tc * 512:(tc + 1) * 512], pair[tc],
                       start=(dc == 0), stop=(dc == 15),
                       extra_waits=([wq_tok[et], t_xblk] if dc == 0 else []))
            xblk_pe_tok = wtok[pair[1]]

            dst = Q_sb if et < 4 else K_sb
            h = et % 4
            for tc in range(2):
                col = tb * 1024 + tc * 512
                read_slot("scalar", pair[tc],
                          lambda E, d=dst, h=h, c=col, sl_=pair[tc], et=et:
                          E.activation(d[:, h, c:c + 512], ps_half(*sl_),
                                       AF.Identity, bias=sm[:, et:et + 1],
                                       scale=1.0),
                          inc="act", extra_waits=[t_sm])

        for tt in range(8):
            slot = ("O",) if vslot % 2 == 0 else ("M",)
            vslot += 1
            pst = psO if slot == ("O",) else psM
            for dc in range(16):
                mm(pst[:, :], xblk[:, dc, tt * 128:(tt + 1) * 128],
                   wq_bf[:, dc, 1024:1536], slot, start=(dc == 0), stop=False,
                   extra_waits=([t_xblk, wq_tok[8], wq_tok[9], wq_tok[10],
                                 wq_tok[11]] if dc == 0 else []))
            mm(pst[:, :], ones_bf[0:1, 0:128], bv_bf[0:1, :], slot,
               start=False, stop=True, extra_waits=[t_bvbf])
            xblk_pe_tok = wtok[slot]
            gt = tb * 8 + tt
            read_slot("scalar", slot,
                      lambda E, gt=gt, p=pst: E.activation(
                          V_sb[:, gt, :], p[:, :], AF.Copy, bias=0.0, scale=1.0),
                      inc="act")

    t_qkv_act = s.cur("act")
    rtok[("O", 0)] = rtok.get(("O",))
    rtok[("M", 1)] = rtok.get(("M",))
    rtok[("O", 1)] = rtok.get(("M",))
    rtok[("M", 0)] = rtok.get(("O",))
    rtok[("M", 0)] = None
    rtok[("M", 1)] = None

    pt_tok = [None, None]
    u_read_tok = None
    c_read_tok = None
    pending_norm = []

    def emit_norm(h, qg, t_rec, t_oev):
        p = qg % 2
        d1 = dma_out(recd.ap()[p:p + 1, :], recs[p][0:1, :], sem=f"rb{p}",
                     extra_waits=[t_rec])
        rtok[("RECF", p)] = d1
        bsrc = bass.AP(recd, p * 512, [[0, 128], [1, 512]])
        d2 = dma_out(recbc[:, p, :], bsrc, sem=f"rb{p}",
                     extra_waits=[rtok.get(("RB", p)), d1])
        s.wait("vector", d2)
        s.wait("vector", t_oev)
        t_onorm = s.op("vector",
                       lambda E, h=h, qg=qg: E.tensor_mul(
                           O_sb[:, h, qg * 512:(qg + 1) * 512],
                           O_sb[:, h, qg * 512:(qg + 1) * 512],
                           recbc[:, qg % 2, :]),
                       inc="dve")
        rtok[("RB", p)] = t_onorm
        return t_onorm

    def flush_norms(upto):
        while len(pending_norm) > upto:
            emit_norm(*pending_norm.pop(0))

    for h in range(4):
        qsq_tok = None
        for side, src in ((0, Q_sb), (1, K_sb)):
            for ch in range(8):
                s.wait("vector", t_qkv_act)
                s.wait("vector", qsq_tok)
                sq_t = s.op("vector",
                            lambda E, src=src, h=h, ch=ch: E.tensor_mul(
                                qsq[:, :], src[:, h, ch * 512:(ch + 1) * 512],
                                src[:, h, ch * 512:(ch + 1) * 512]),
                            inc="dve")
                s.wait("tensor", u_read_tok)
                s.wait("tensor", rtok.get(("M", 0)))
                s.wait("tensor", rtok.get(("M", 1)))
                s.wait("tensor", sq_t)
                pe_t = s.op("tensor",
                            lambda E: E.matmul(psC[0:1, :], ones_bf[:, 0:1],
                                               qsq[:, :], start=True, stop=True),
                            inc="pe")
                qsq_tok = pe_t
                s.wait("vector", pe_t)
                rm_t = s.op("vector",
                            lambda E: E.reduce_max(uw[0:1, 4:5], psC[0:1, :],
                                                   axis=AX.X),
                            inc="dve")
                u_read_tok = rm_t
                s.wait("vector", rm_t)
                if ch == 0:
                    s.op("vector", lambda E, sd=side: E.tensor_copy(
                        uw[0:1, sd:sd + 1], uw[0:1, 4:5]), inc="dve")
                else:
                    s.op("vector", lambda E, sd=side: E.tensor_max(
                        uw[0:1, sd:sd + 1], uw[0:1, sd:sd + 1], uw[0:1, 4:5]),
                        inc="dve")
        s.wait("vector", s.cur("dve"))
        t_prod = s.op("vector", lambda E: E.tensor_mul(
            uw[0:1, 2:3], uw[0:1, 0:1], uw[0:1, 1:2]), inc="dve")
        s.wait("scalar", t_prod)
        t_ln = s.op("scalar", lambda E: E.activation(
            uw[0:1, 3:4], uw[0:1, 2:3], AF.Ln), inc="act")
        s.wait("scalar", t_ln)
        t_u = s.op("scalar", lambda E: E.activation(
            uw[0:1, 5:6], uw[0:1, 3:4], AF.Exp, bias=uw[0:1, 6:7],
            scale=0.5), inc="act")
        rtok[("M", 0)] = u_read_tok
        s.wait("vector", t_u)
        t_nub = s.op("vector", lambda E: E.tensor_scalar(
            nub[0:1, 0:1], uw[0:1, 5:6], -1.0, None, ALU.mult), inc="dve")
        s.wait("tensor", t_nub)
        s.wait("tensor", rtok.get(("BCU",)))
        t_bc = s.op("tensor", lambda E: E.matmul(
            psU[:, 0:1], ones_bf[0:1, :], nub[0:1, 0:1], start=True, stop=True),
            inc="pe")
        s.wait("scalar", t_bc)
        t_negu = s.op("scalar", lambda E: E.activation(
            negU[:, 0:1], psU[:, 0:1], AF.Copy, bias=0.0, scale=1.0), inc="act")
        s.wait("scalar", t_negu)
        rtok[("BCU",)] = t_negu

        for qg in range(8):
            npairs = 2 * qg + 2
            exp_toks = {}
            mask_toks = {}
            flush_norms(1)
            for pr in range(npairs + 1):
                if pr < npairs:
                    pst = psA if pr % 2 == 0 else psB
                    slot = ("S", pr % 2)
                    for half in range(2):
                        kt = 2 * pr + half
                        mm(pst[:, half * 512:(half + 1) * 512],
                           K_sb[:, h, kt * 128:(kt + 1) * 128],
                           Q_sb[:, h, qg * 512:(qg + 1) * 512],
                           slot, start=True, stop=True,
                           extra_waits=[t_qkv_act] if half == 0 else [])
                    read_slot("scalar", slot,
                              lambda E, pst=pst, pr=pr: E.activation(
                                  PT[:, pr % 2, :], pst[:, :], AF.Exp,
                                  bias=negU[:, 0:1], scale=INV_SQRT_HD),
                              inc="act", extra_waits=[pt_tok[pr % 2]])
                    exp_toks[pr] = s.cur("act")
                    for half in range(2):
                        kt = 2 * pr + half
                        v = kt - 4 * qg
                        if 0 <= v <= 3:
                            s.wait("vector", exp_toks[pr])
                            mt = s.op("vector",
                                      lambda E, pr=pr, half=half, v=v:
                                      E.tensor_mul(
                                          PT[:, pr % 2,
                                             half * 512:(half + 1) * 512],
                                          PT[:, pr % 2,
                                             half * 512:(half + 1) * 512],
                                          masks[:, v, :]),
                                      inc="dve")
                            mask_toks[(pr, half)] = mt
                if pr > 0:
                    ppr = pr - 1
                    last = (ppr == npairs - 1)
                    po = psO if qg % 2 == 0 else psM
                    oslot = ("O", qg % 2)
                    mslot = ("M", qg % 2)
                    for half in range(2):
                        kt = 2 * ppr + half
                        ew = [exp_toks[ppr]]
                        if (ppr, half) in mask_toks:
                            ew.append(mask_toks[(ppr, half)])
                        mm(po[:, :], V_sb[:, kt, h * 128:(h + 1) * 128],
                           PT[:, ppr % 2, half * 512:(half + 1) * 512],
                           oslot, start=(kt == 0), stop=(last and half == 1),
                           extra_waits=ew)
                        mm(psC[32 * (qg % 2):32 * (qg % 2) + 1, :],
                           ones_bf[:, 0:1],
                           PT[:, ppr % 2, half * 512:(half + 1) * 512],
                           mslot, start=(kt == 0), stop=(last and half == 1),
                           extra_waits=[], inc_anyway=(half == 1))
                    pt_tok[ppr % 2] = s.cur("pe")
            po = psO if qg % 2 == 0 else psM
            oslot = ("O", qg % 2)
            mslot = ("M", qg % 2)
            s.wait("vector", wtok[mslot])
            s.wait("vector", rtok.get(("RECF", qg % 2)))
            t_rec = s.op("vector", lambda E, qg=qg: E.reciprocal(
                recs[qg % 2][0:1, :], psC[32 * (qg % 2):32 * (qg % 2) + 1, :]),
                inc="dve")
            rtok[mslot] = t_rec
            t_oev = read_slot("scalar", oslot,
                              lambda E, h=h, qg=qg, po=po: E.activation(
                                  O_sb[:, h, qg * 512:(qg + 1) * 512],
                                  po[:, :], AF.Copy, bias=0.0, scale=1.0),
                              inc="act")
            pending_norm.append((h, qg, t_rec, t_oev))

    flush_norms(0)
    t_attn_dve = s.cur("dve")

    wout_tok = None
    for dc in range(4):
        tl = dma_in(xstage[:, dc % 2, :], wo[dc], sem=f"x{dc % 2}",
                    extra_waits=[xstage_tok[dc % 2]])
        s.wait("vector", tl)
        ct = s.op("vector",
                  lambda E, dc=dc: E.tensor_copy(wout_bf[:, dc, :],
                                                 xstage[:, dc % 2, :]),
                  inc="dve")
        xstage_tok[dc % 2] = ct
        wout_tok = ct

    cs_tok = [None] * 4
    op_slots = [("A", 0), ("A", 1), ("B", 0), ("B", 1)]
    ns = 0
    cseq = 0
    t_cc = {}
    for idx, tcg in enumerate([0, 2, 4, 6, 1, 3, 5, 7]):
        half = tcg % 2
        for et in range(16):
            slot = op_slots[ns % 4]
            for dc in range(4):
                mm(ps_half(*slot), wout_bf[:, dc, et * 128:(et + 1) * 128],
                   O_sb[:, dc, tcg * 512:(tcg + 1) * 512], slot,
                   start=(dc == 0), stop=(dc == 3),
                   extra_waits=([wout_tok, t_attn_dve] if dc == 0 else []))
            sl = cseq % 4
            ns += 1
            cseq += 1
            if et % 2 == 0:
                ev = read_slot("scalar", slot,
                               lambda E, sl=sl, s_=slot: E.activation(
                                   cstage[:, sl, :], ps_half(*s_), AF.Copy,
                                   bias=0.0, scale=1.0),
                               inc="act", extra_waits=[cs_tok[sl]])
            else:
                ev = read_slot("vector", slot,
                               lambda E, sl=sl, s_=slot: E.tensor_copy(
                                   cstage[:, sl, :], ps_half(*s_)),
                               inc="dve", extra_waits=[cs_tok[sl]])
            r0 = (tcg // 2) * 2048 + et * 128
            dst = cbuf2[half].ap()[r0:r0 + 128, :]
            if sl % 2 == 0:
                cs_tok[sl] = dma_out(dst, cstage[:, sl, :], sem=f"c{sl}",
                                     extra_waits=[ev])
            else:
                cs_tok[sl] = dma_in(dst, cstage[:, sl, :], sem=f"c{sl}",
                                    extra_waits=[ev])
        if idx == 3 or idx == 7:
            h_ = 0 if idx == 3 else 1
            for cn in ("c0", "c1", "c2", "c3"):
                s.wait("gpsimd", s.cur(cn))
            t_cc[h_] = s.op("gpsimd", lambda E, h_=h_: E.collective_compute(
                "ReduceScatter", ALU.add,
                replica_groups=[[0, 1, 2, 3], [4, 5, 6, 7]],
                ins=[cbuf2[h_].ap().opt()], outs=[rbuf2[h_].ap().opt()]),
                inc="cc")
        if idx == 3:
            t_op_pe_evens = s.cur("pe")
    t_op_pe = s.cur("pe")

    up_pref = []
    for i in range(2):
        up_pref.append(dma_in(ustage[:, i, :], up[i], sem=f"us{i}",
                              extra_waits=[t_op_pe]))
    s.wait("gpsimd", t_cc[1])

    def layer_norm(src_big, rb, sq, tt1, mbc, rbc, out_cb):
        slots_ps = [psO, psM, psC, psU]
        rb_tok = [None, None]
        for et in range(16):
            sl = et % 2
            s.wait("vector", rb_tok[sl])
            s.wait("vector", rb_tok[1 - sl])
            tcast = s.op("vector",
                         lambda E, et=et, sl=sl, rb=rb, src=src_big:
                         E.tensor_copy(rb[:, sl, :], src[:, et, :]), inc="dve")
            s.wait("vector", tcast)
            tsq = s.op("vector",
                       lambda E, sl=sl, rb=rb, sq=sq: E.tensor_mul(
                           sq[:, :], rb[:, sl, :], rb[:, sl, :]), inc="dve")
            for half in range(2):
                mm(slots_ps[half][0:1, :], ones_bf[:, 0:1],
                   rb[:, sl, half * 512:(half + 1) * 512], ("LNM", half),
                   start=(et == 0), stop=(et == 15), extra_waits=[tcast],
                   inc_anyway=True)
            for half in range(2):
                mm(slots_ps[2 + half][0:1, :], ones_bf[:, 0:1],
                   sq[:, half * 512:(half + 1) * 512], ("LNV", half),
                   start=(et == 0), stop=(et == 15), extra_waits=[tsq],
                   inc_anyway=True)
            rb_tok[sl] = s.cur("pe")
        t_pe_rows = s.cur("pe")
        s.wait("vector", t_pe_rows)
        for half in range(2):
            s.op("vector", lambda E, half=half: E.tensor_scalar(
                rows[0:1, half * 512:half * 512 + 512],
                slots_ps[half][0:1, :], 1.0 / D, None, ALU.mult), inc="dve")
            s.op("vector", lambda E, half=half: E.tensor_scalar(
                rows[0:1, 1024 + half * 512:1024 + half * 512 + 512],
                slots_ps[2 + half][0:1, :], 1.0 / D, None, ALU.mult), inc="dve")
        s.wait("vector", s.cur("dve"))
        s.op("vector", lambda E: E.tensor_mul(
            rows[0:1, 2048:3072], rows[0:1, 0:1024], rows[0:1, 0:1024]),
            inc="dve")
        s.wait("vector", s.cur("dve"))
        t_var = s.op("vector", lambda E: E.tensor_sub(
            rows[0:1, 1024:2048], rows[0:1, 1024:2048], rows[0:1, 2048:3072]),
            inc="dve")
        s.wait("vector", t_var)
        t_var = s.op("vector", lambda E: E.tensor_scalar(
            rows[0:1, 1024:2048], rows[0:1, 1024:2048], LN_EPS, None, ALU.add),
            inc="dve")
        for key in (("LNM", 0), ("LNM", 1), ("LNV", 0), ("LNV", 1)):
            rtok[key] = t_var
        s.wait("scalar", t_var)
        t_lnv = s.op("scalar", lambda E: E.activation(
            rows[0:1, 2048:3072], rows[0:1, 1024:2048], AF.Ln), inc="act")
        s.wait("scalar", t_lnv)
        t_rstd = s.op("scalar", lambda E: E.activation(
            rows[0:1, 3072:4096], rows[0:1, 2048:3072], AF.Exp, scale=-0.5),
            inc="act")
        s.wait("vector", t_rstd)
        s.op("vector", lambda E: E.tensor_mul(
            rows[0:1, 2048:3072], rows[0:1, 3072:4096], rows[0:1, 3072:4096]),
            inc="dve")
        s.wait("vector", s.cur("dve"))
        s.op("vector", lambda E: E.tensor_mul(
            rows[0:1, 2048:3072], rows[0:1, 2048:3072], rows[0:1, 1024:2048]),
            inc="dve")
        s.wait("vector", s.cur("dve"))
        s.op("vector", lambda E: E.tensor_scalar(
            rows[0:1, 2048:3072], rows[0:1, 2048:3072], -0.5, 1.5,
            ALU.mult, ALU.add), inc="dve")
        s.wait("vector", s.cur("dve"))
        t_rstd = s.op("vector", lambda E: E.tensor_mul(
            rows[0:1, 3072:4096], rows[0:1, 3072:4096], rows[0:1, 2048:3072]),
            inc="dve")
        t_bc_done = None
        for half in range(2):
            for ri, dst in ((0, mbc), (3, rbc)):
                src_row = rows[0:1,
                               ri * 1024 + half * 512:ri * 1024 + half * 512 + 512]
                s.wait("tensor", t_rstd if ri == 3 else t_var)
                s.wait("tensor", rtok.get(("LNBC",)))
                tbc = s.op("tensor",
                           lambda E, sr=src_row: E.matmul(
                               psC[:, :], ones_f[0:1, :], sr, start=True,
                               stop=True), inc="pe")
                s.wait("scalar", tbc)
                tcp = s.op("scalar",
                           lambda E, dst=dst, half=half: E.activation(
                               dst[:, half * 512:(half + 1) * 512], psC[:, :],
                               AF.Copy, bias=0.0, scale=1.0), inc="act")
                rtok[("LNBC",)] = tcp
                t_bc_done = tcp
        for et in range(16):
            s.wait("vector", t_bc_done)
            tsub = s.op("vector", lambda E, et=et, tt1=tt1, src=src_big, mbc=mbc:
                        E.tensor_sub(tt1[:, :], src[:, et, :], mbc[:, :]),
                        inc="dve")
            s.wait("vector", tsub)
            out_cb(et)
            s.wait("vector", s.cur("dve"))
        return t_bc_done

    xpl_tok = [None, None]
    xln_done = [None] * 2

    r_tok = [None, None]
    xr_tok = [None, None]
    rb_tok = [None, None]

    def ln_half(half, t_ready_tok):
        c0 = half * 512
        for et in range(16):
            sl = et % 2
            ew = [r_tok[sl]]
            if et < 2:
                ew.append(t_ready_tok)
            tl1 = dma_in(rstage[:, sl, 0:512],
                         rbuf2[half].ap()[et * 128:(et + 1) * 128, :],
                         sem=f"r{sl}", extra_waits=ew)
            tl2 = dma_in(xrstage[:, sl, :], xres[et][:, c0:c0 + 512],
                         sem=f"xr{sl}", extra_waits=[xr_tok[sl]])
            s.wait("vector", tl1)
            s.wait("vector", tl2)
            s.wait("vector", t_sm)
            s.wait("vector", rb_tok[sl])
            s.wait("vector", rb_tok[1 - sl])
            tres = s.op("vector",
                        lambda E, et=et, sl=sl, c0=c0: E.scalar_tensor_tensor(
                            bigf[:, et, c0:c0 + 512], rstage[:, sl, 0:512],
                            sm[:, 8 + et:9 + et], xrstage[:, sl, :],
                            ALU.add, ALU.add),
                        inc="dve")
            r_tok[sl] = tres
            xr_tok[sl] = tres
            s.wait("vector", tres)
            tcast = s.op("vector",
                         lambda E, et=et, sl=sl, c0=c0: E.tensor_copy(
                             rb16[:, sl, 0:512], bigf[:, et, c0:c0 + 512]),
                         inc="dve")
            s.wait("vector", tcast)
            tsq = s.op("vector",
                       lambda E, sl=sl: E.tensor_mul(
                           sqb[:, 0:512], rb16[:, sl, 0:512],
                           rb16[:, sl, 0:512]), inc="dve")
            mm(psO[0:1, :], ones_bf[:, 0:1], rb16[:, sl, 0:512],
               ("LNM", half), start=(et == 0), stop=(et == 15),
               extra_waits=[tcast], inc_anyway=True)
            mm(psM[0:1, :], ones_bf[:, 0:1], sqb[:, 0:512],
               ("LNV", half), start=(et == 0), stop=(et == 15),
               extra_waits=[tsq], inc_anyway=True)
            rb_tok[sl] = s.cur("pe")
        t_pe_rows = s.cur("pe")
        s.wait("vector", t_pe_rows)
        R = rows[0:1, :]
        mrow = rows[0:1, c0:c0 + 512]
        vrow = rows[0:1, 1024 + c0:1024 + c0 + 512]
        srow = rows[0:1, 2048 + c0:2048 + c0 + 512]
        rrow = rows[0:1, 3072 + c0:3072 + c0 + 512]
        s.op("vector", lambda E: E.tensor_scalar(
            mrow, psO[0:1, :], 1.0 / D, None, ALU.mult), inc="dve")
        s.op("vector", lambda E: E.tensor_scalar(
            vrow, psM[0:1, :], 1.0 / D, None, ALU.mult), inc="dve")
        s.wait("vector", s.cur("dve"))
        s.op("vector", lambda E: E.tensor_mul(srow, mrow, mrow), inc="dve")
        s.wait("vector", s.cur("dve"))
        s.op("vector", lambda E: E.tensor_sub(vrow, vrow, srow), inc="dve")
        s.wait("vector", s.cur("dve"))
        t_var = s.op("vector", lambda E: E.tensor_scalar(
            vrow, vrow, LN_EPS, None, ALU.add), inc="dve")
        rtok[("LNM", half)] = t_var
        rtok[("LNV", half)] = t_var
        s.wait("scalar", t_var)
        t_lnv = s.op("scalar", lambda E: E.activation(srow, vrow, AF.Ln),
                     inc="act")
        s.wait("scalar", t_lnv)
        t_rstd = s.op("scalar", lambda E: E.activation(
            rrow, srow, AF.Exp, scale=-0.5), inc="act")
        s.wait("vector", t_rstd)
        s.op("vector", lambda E: E.tensor_mul(srow, rrow, rrow), inc="dve")
        s.wait("vector", s.cur("dve"))
        s.op("vector", lambda E: E.tensor_mul(srow, srow, vrow), inc="dve")
        s.wait("vector", s.cur("dve"))
        s.op("vector", lambda E: E.tensor_scalar(
            srow, srow, -0.5, 1.5, ALU.mult, ALU.add), inc="dve")
        s.wait("vector", s.cur("dve"))
        t_rstd = s.op("vector", lambda E: E.tensor_mul(rrow, rrow, srow),
                      inc="dve")
        s.wait("tensor", t_var)
        s.wait("tensor", rtok.get(("LNBC",)))
        tb1 = s.op("tensor", lambda E: E.matmul(
            psC[:, :], ones_f[0:1, :], mrow, start=True, stop=True), inc="pe")
        s.wait("scalar", tb1)
        cm1 = s.op("scalar", lambda E: E.activation(
            meanbc[:, 0:512], psC[:, :], AF.Copy, bias=0.0, scale=1.0),
            inc="act")
        s.wait("tensor", t_rstd)
        s.wait("tensor", rtok.get(("LNBC2",)))
        tb2 = s.op("tensor", lambda E: E.matmul(
            psU[:, :], ones_f[0:1, :], rrow, start=True, stop=True), inc="pe")
        s.wait("scalar", tb2)
        cm2 = s.op("scalar", lambda E: E.activation(
            rstdbc[:, 0:512], psU[:, :], AF.Copy, bias=0.0, scale=1.0),
            inc="act")
        rtok[("LNBC",)] = cm1
        rtok[("LNBC2",)] = cm2
        for et in range(16):
            sl = et % 2
            s.wait("vector", cm1)
            s.wait("vector", cm2)
            tsub = s.op("vector",
                        lambda E, et=et, c0=c0: E.tensor_sub(
                            t1[:, 0:512], bigf[:, et, c0:c0 + 512],
                            meanbc[:, 0:512]), inc="dve")
            s.wait("vector", tsub)
            tk1 = s.op("vector", lambda E, et=et, c0=c0: E.tensor_mul(
                xln[:, et, c0:c0 + 512], t1[:, 0:512], rstdbc[:, 0:512]),
                inc="dve")
            s.wait("vector", xpl_tok[sl])
            tk2 = s.op("vector", lambda E, sl=sl, c0=c0: E.tensor_mul(
                xplstage[:, sl, 0:512], t1[:, 0:512], rstdbc[:, 0:512]),
                inc="dve")
            s.wait("vector", s.cur("dve"))
            xpl_tok[sl] = dma_out(xln1d[et][:, c0:c0 + 512],
                                  xplstage[:, sl, 0:512], sem=f"xo{sl}",
                                  extra_waits=[tk2])
        xln_done[half] = s.cur("dve")

    ln_half(0, t_cc[0])
    ln_half(1, t_cc[1])
    t_ln1 = s.cur("dve")
    t_xpl_w0 = s.cur("xo0")
    t_xpl_w1 = s.cur("xo1")

    us_tok = [None, None]
    ub_read = [None, None]
    sign_tok = [None, None]
    ds_tok = [None, None]
    dn_read = [None] * 8
    hg_read = [None] * 8
    acc_tok = [[None, None] for _ in range(16)]
    hd_slots = [("A", 0), ("A", 1)]
    dd_slots = [("B", 0), ("B", 1)]

    for g in range(8):
        for hci in range(8):
            ht = g * 8 + hci
            usl = ht % 2
            if ht < 2:
                tl = up_pref[ht]
            else:
                tl = dma_in(ustage[:, usl, :], up[ht], sem=f"us{usl}",
                            extra_waits=[sign_tok[usl]])
            s.wait("scalar", tl)
            s.wait("scalar", ub_read[usl])
            tsg = s.op("scalar",
                       lambda E, usl=usl: E.activation(
                           ub16[:, usl, :], ustage[:, usl, :], AF.Sign),
                       inc="act")
            sign_tok[usl] = tsg
            for tc in range(2):
                for dc in range(16):
                    ew = []
                    if dc == 0:
                        ew = [tsg, xln_done[tc]]
                    mm(ps_half(*hd_slots[tc]),
                       ub16[:, usl, dc * 128:(dc + 1) * 128],
                       xln[:, dc, tc * 512:(tc + 1) * 512], hd_slots[tc],
                       start=(dc == 0), stop=(dc == 15), extra_waits=ew)
            ub_read[usl] = wtok[hd_slots[1]]
            for tc in range(2):
                read_slot("vector", hd_slots[tc],
                          lambda E, hci=hci, tc=tc, ht=ht, s_=hd_slots[tc]:
                          E.tensor_scalar(
                              hgrp[:, hci, tc * 512:(tc + 1) * 512],
                              ps_half(*s_), sm[:, 24 + ht:25 + ht], 0.0,
                              ALU.mult, ALU.max),
                          inc="dve", extra_waits=[hg_read[hci], t_sm])
        t_relu_g = s.cur("dve")

        for hci in range(8):
            dsl = hci % 2
            ew = [ds_tok[dsl]]
            if g == 0 and hci < 2:
                ew += [t_ln1, t_xpl_w0, t_xpl_w1]
            tl = dma_in(dstage[:, dsl, :], down[g, hci], sem=f"dd{dsl}",
                        extra_waits=ew)
            s.wait("scalar", tl)
            s.wait("scalar", dn_read[hci])
            if g == 0 and hci < 2:
                s.wait("scalar", t_xpl_w0)
                s.wait("scalar", t_xpl_w1)
                s.wait("scalar", t_ln1)
            tsg = s.op("scalar",
                       lambda E, hci=hci, dsl=dsl: E.activation(
                           dn16[:, hci, :, :].rearrange("p a b -> p (a b)"),
                           dstage[:, dsl, :], AF.Sign),
                       inc="act")
            ds_tok[dsl] = tsg
        t_sign_g = s.cur("act")

        for et in range(16):
            for tc in range(2):
                for hci in range(8):
                    ew = []
                    if hci == 0:
                        ew = [t_sign_g, t_relu_g]
                    mm(ps_half(*dd_slots[tc]), dn16[:, hci, et, :],
                       hgrp[:, hci, tc * 512:(tc + 1) * 512], dd_slots[tc],
                       start=(hci == 0), stop=(hci == 7), extra_waits=ew)
            for tc in range(2):
                if g == 0:
                    fn = (lambda E, et=et, tc=tc, s_=dd_slots[tc]:
                          E.tensor_copy(bigf[:, et, tc * 512:(tc + 1) * 512],
                                        ps_half(*s_)))
                else:
                    fn = (lambda E, et=et, tc=tc, s_=dd_slots[tc]:
                          E.tensor_add(bigf[:, et, tc * 512:(tc + 1) * 512],
                                       bigf[:, et, tc * 512:(tc + 1) * 512],
                                       ps_half(*s_)))
                read_slot("vector", dd_slots[tc], fn, inc="dve")
                acc_tok[et][tc] = s.cur("dve")
        t_down_g = s.cur("pe")
        for hci in range(8):
            hg_read[hci] = t_down_g
            dn_read[hci] = t_down_g
    t_down_pe = s.cur("pe")
    t_sign_all = s.cur("act")

    x2_tok = [None, None]
    for et in range(16):
        sl = et % 2
        ew = [x2_tok[sl]]
        if et < 2:
            ew += [t_down_pe, t_sign_all, t_xpl_w0, t_xpl_w1]
        tl = dma_in(xpl2[:, sl, :], xln1d[et], sem=f"p{sl}", extra_waits=ew)
        s.wait("vector", tl)
        s.wait("vector", t_dsos)
        tres = s.op("vector",
                    lambda E, et=et, sl=sl: E.scalar_tensor_tensor(
                        bigf[:, et, :], bigf[:, et, :], dsos[:, et:et + 1],
                        xpl2[:, sl, :], ALU.mult, ALU.add),
                    inc="dve")
        x2_tok[sl] = tres

    f_tok = [None, None]
    y_writes = []

    def ln2_out(et):
        sl = et % 2
        s.wait("vector", f_tok[sl])
        tk = s.op("vector", lambda E, sl=sl: E.tensor_mul(
            fstage[:, sl, :], t1b[:, :], rstdbc2[:, :]), inc="dve")
        dt = dma_out(y[et], fstage[:, sl, :], sem=f"yo{sl}",
                     extra_waits=[tk])
        f_tok[sl] = dt
        y_writes.append(dt)

    layer_norm(bigf, rb16b, sqb2, t1b, meanbc2, rstdbc2, ln2_out)

    s.wait("gpsimd", s.cur("yo0"))
    s.wait("gpsimd", s.cur("yo1"))

    with nc.Block() as block:
        def runner(eng_name):
            def go(E):
                for item in s.ops[eng_name]:
                    if item[0] == "w":
                        _, sname, v = item
                        E.wait_ge(sems[sname], v)
                    else:
                        _, fn, inc, n = item
                        ins = fn(E)
                        if inc is not None:
                            ins.then_inc(sems[inc], n)
            return go

        block.sync(runner("sync"))
        block.gpsimd(runner("gpsimd"))
        block.vector(runner("vector"))
        block.scalar(runner("scalar"))
        block.tensor(runner("tensor"))

    return nc


def _prep_inputs(L0_attn_x, L0_attn_w_qkv, L0_attn_b_qkv, L0_attn_w_out,
                 L0_attn_b_out, L0_ffn_up_weight, L0_ffn_down_weight,
                 L0_ffn_up_scale, L0_ffn_down_scale, L0_ffn_output_scale):
    x = np.asarray(L0_attn_x, np.float32)
    wqkv = np.asarray(L0_attn_w_qkv, np.float32)
    bqkv = np.asarray(L0_attn_b_qkv, np.float32)
    wout = np.asarray(L0_attn_w_out, np.float32)
    bout = np.asarray(L0_attn_b_out, np.float32)
    upw = np.asarray(L0_ffn_up_weight, np.float32)
    dnw = np.asarray(L0_ffn_down_weight, np.float32)
    usc = np.asarray(L0_ffn_up_scale, np.float32)
    dsc = np.asarray(L0_ffn_down_scale, np.float32)
    osc = np.asarray(L0_ffn_output_scale, np.float32).reshape(1)

    up_p = np.ascontiguousarray(
        upw.reshape(64, 128, 16, 128).transpose(0, 3, 2, 1)).reshape(
        64, 128, 2048)
    down_p = np.ascontiguousarray(dnw.T).reshape(8, 8, 128, 2048)
    woutT = np.ascontiguousarray(wout.T)

    in_maps = []
    xq_cache = {}
    xt_cache = {}
    for c in range(8):
        b, j = c // 4, c % 4
        if b not in xq_cache:
            xq_cache[b] = np.ascontiguousarray(
                x[b].reshape(32, 128, 16, 128).transpose(0, 3, 2, 1)
            ).reshape(32, 128, 2048)
            xt_cache[b] = np.ascontiguousarray(x[b].T)
        rows = []
        for sec in range(3):
            base = sec * 2048 + j * 512
            rows.append(np.arange(base, base + 512))
        rows_idx = np.concatenate(rows)
        wsl = wqkv[rows_idx]
        wq_p = np.ascontiguousarray(
            wsl.reshape(12, 128, 16, 128).transpose(0, 3, 2, 1)).reshape(
            12, 128, 2048)
        bqk = bqkv[rows_idx[:1024]].reshape(8, 128).T
        bv_r = np.ascontiguousarray(bqkv[rows_idx[1024:]].reshape(1, 512))
        wout_p = np.ascontiguousarray(woutT[j * 512:(j + 1) * 512]).reshape(
            4, 128, 2048)
        tok0 = j * 1024
        xres_p = np.ascontiguousarray(
            xt_cache[b][:, tok0:tok0 + 1024]).reshape(16, 128, 1024)
        sm_a = np.zeros((128, 112), np.float32)
        sm_a[:, 0:8] = bqk
        sm_a[:, 8:24] = bout.reshape(16, 128).T
        sm_a[:, 24:88] = usc.reshape(64, 128).T
        sm_a[:, 88:104] = dsc.reshape(16, 128).T
        sm_a[:, 104] = osc[0]
        in_maps.append({
            "xq": xq_cache[b], "wq": wq_p, "wo": wout_p, "xres": xres_p,
            "up": up_p, "down": down_p, "smalls": sm_a, "bv": bv_r,
        })
    return in_maps


_NC_CACHE = {}


def kernel(**inputs):
    in_maps = _prep_inputs(**inputs)
    if "nc" not in _NC_CACHE:
        _NC_CACHE["nc"] = build_program()
    nc = _NC_CACHE["nc"]
    res = run_bass_kernel_spmd(nc, in_maps, core_ids=list(range(8)))
    out = np.empty((2, 4096, 2048), np.float32)
    for c in range(8):
        b, j = c // 4, c % 4
        yc = np.asarray(res.results[c]["y"])
        out[b, j * 1024:(j + 1) * 1024, :] = yc.transpose(2, 0, 1).reshape(
            1024, 2048)
    return out


# version 16
# speedup vs baseline: 1.0828x; 1.0828x over previous
import math
import numpy as np

import concourse.bass as bass
import concourse.mybir as mybir
from concourse.bass_utils import run_bass_kernel_spmd

F32 = mybir.dt.float32
BF16 = mybir.dt.bfloat16
AF = mybir.ActivationFunctionType
ALU = mybir.AluOpType
AX = mybir.AxisListType

S = 4096
D = 2048
HD = 128
INV_SQRT_HD = 1.0 / math.sqrt(HD)
LN_EPS = 1e-5

ENGS = ("sync", "gpsimd", "vector", "scalar", "tensor")
PART_BYTES = 229344


class Sched:

    def __init__(self):
        self.ops = {e: [] for e in ENGS}
        self.cnt = {}
        self.waited = {e: {} for e in ENGS}

    def wait(self, eng, tok):
        if tok is None:
            return
        sname, v = tok
        if v <= 0 or self.waited[eng].get(sname, 0) >= v:
            return
        self.waited[eng][sname] = v
        self.ops[eng].append(("w", sname, v))

    def op(self, eng, fn, inc=None, n=1):
        tok = None
        if inc is not None:
            self.cnt[inc] = self.cnt.get(inc, 0) + n
            tok = (inc, self.cnt[inc])
        self.ops[eng].append(("o", fn, inc, n))
        return tok

    def cur(self, sem):
        return (sem, self.cnt.get(sem, 0))


def build_program():
    nc = bass.Bass("TRN2", target_bir_lowering=False, debug=False)

    xq = nc.dram_tensor("xq", [32, 128, 2048], F32, kind="ExternalInput").ap()
    wq = nc.dram_tensor("wq", [12, 128, 2048], F32, kind="ExternalInput").ap()
    wo = nc.dram_tensor("wo", [4, 128, 2048], F32, kind="ExternalInput").ap()
    xres = nc.dram_tensor("xres", [16, 128, 1024], F32, kind="ExternalInput").ap()
    up = nc.dram_tensor("up", [64, 128, 2048], F32, kind="ExternalInput").ap()
    down = nc.dram_tensor("down", [8, 8, 128, 2048], F32,
                          kind="ExternalInput").ap()
    smalls = nc.dram_tensor("smalls", [128, 112], F32, kind="ExternalInput").ap()
    bv = nc.dram_tensor("bv", [1, 512], F32, kind="ExternalInput").ap()
    y = nc.dram_tensor("y", [16, 128, 1024], F32, kind="ExternalOutput").ap()

    cbuf2 = [nc.dram_tensor("cbufa", [8192, 512], BF16),
             nc.dram_tensor("cbufb", [8192, 512], BF16)]
    rbuf2 = [nc.dram_tensor("rbufa", [2048, 512], BF16),
             nc.dram_tensor("rbufb", [2048, 512], BF16)]
    xln1d = nc.dram_tensor("xln1d", [16, 128, 1024], F32)
    recd = nc.dram_tensor("recd", [2, 512], F32)

    def sb(name, shape, dtype, off):
        per = int(np.prod(shape[1:])) * mybir.dt.size(dtype)
        assert off + per <= PART_BYTES, f"{name} overflows: {off}+{per}"
        return nc.alloc_sbuf_tensor_at(name, shape, dtype, offset=off)

    ones_bf = sb("ones_bf", [128, 128], BF16, 16512)
    ones_f = sb("ones_f", [1, 128], F32, 16768)
    masks = sb("masks", [128, 4, 512], BF16, 17280)
    sm = sb("sm", [128, 112], F32, 21376)
    dsos = sb("dsos", [128, 16], F32, 21824)
    bv_sb = sb("bv_sb", [1, 512], F32, 21888)
    bv_bf = sb("bv_bf", [1, 512], BF16, 23936)
    negU = sb("negU", [128, 1], F32, 24960)
    uw = sb("uw", [1, 16], F32, 24992)
    nub = sb("nub", [1, 1], BF16, 25056)
    rec_f = sb("rec_f", [1, 512], F32, 25088)
    rec_f1 = sb("rec_f1", [1, 512], F32, 21888)
    recs = [rec_f, rec_f1]
    recbc = sb("recbc", [128, 2, 512], BF16, 27136)
    qsq = sb("qsq", [128, 512], BF16, 29184)
    Q_sb = sb("Q_sb", [128, 4, 4096], BF16, 30208)
    K_sb = sb("K_sb", [128, 4, 4096], BF16, 62976)
    V_sb = sb("V_sb", [128, 32, 512], BF16, 95744)
    wq_bf = sb("wq_bf", [128, 16, 1536], BF16, 128512)
    xstage = sb("xstage", [128, 2, 2048], F32, 177664)
    xblk = sb("xblk", [128, 16, 512], BF16, 194048)
    O_sb = sb("O_sb", [128, 4, 4096], BF16, 128512)
    PT = sb("PT", [128, 2, 1024], BF16, 161280)
    cstage = sb("cstage", [128, 4, 512], BF16, 165376)
    wout_bf = sb("wout_bf", [128, 4, 2048], BF16, 194048)
    bigf = sb("bigf", [128, 16, 1024], F32, 30208)
    xln = sb("xln", [128, 16, 1024], BF16, 95744)
    ustage = sb("ustage", [128, 2, 2048], F32, 128512)
    wst = [sb("wst_a", [128, 2048], F32, 212480),
           sb("wst_b", [128, 2048], F32, 220672)]
    ub16 = sb("ub16", [128, 2, 2048], BF16, 144896)
    rows = sb("rows", [1, 4096], F32, 212480)
    rstage = sb("rstage", [128, 2, 1024], BF16, 153088)
    xrstage = sb("xrstage", [128, 2, 512], F32, 157184)
    rb16 = sb("rb16", [128, 2, 1024], BF16, 169472)
    sqb = sb("sqb", [128, 1024], BF16, 173568)
    t1 = sb("t1", [128, 1024], F32, 175616)
    meanbc = sb("meanbc", [128, 1024], F32, 179712)
    rstdbc = sb("rstdbc", [128, 1024], F32, 183808)
    xplstage = sb("xplstage", [128, 2, 1024], F32, 187904)
    dstage = sb("dstage", [128, 2, 2048], F32, 196096)
    dn16 = sb("dn16", [128, 8, 16, 128], BF16, 153088)
    hgrp = sb("hgrp", [128, 8, 1024], BF16, 212480)
    xpl2 = sb("xpl2", [128, 2, 1024], F32, 153088)
    rb16b = sb("rb16b", [128, 2, 1024], BF16, 161280)
    sqb2 = sb("sqb2", [128, 1024], BF16, 165376)
    t1b = sb("t1b", [128, 1024], F32, 167424)
    meanbc2 = sb("meanbc2", [128, 1024], F32, 171520)
    rstdbc2 = sb("rstdbc2", [128, 1024], F32, 175616)
    fstage = sb("fstage", [128, 2, 1024], F32, 179712)
    psA = nc.alloc_psum_tensor("psA", [128, 1024], F32)
    psB = nc.alloc_psum_tensor("psB", [128, 1024], F32)
    psO = nc.alloc_psum_tensor("psO", [128, 512], F32)
    psM = nc.alloc_psum_tensor("psM", [128, 512], F32)
    psC = nc.alloc_psum_tensor("psC", [128, 512], F32)
    psU = nc.alloc_psum_tensor("psU", [128, 512], F32)

    sems = {n: nc.alloc_semaphore(n) for n in
            ("ld", "st", "pe", "act", "dve", "pool", "cc",
             "x0", "x1", "r0", "r1", "xr0", "xr1", "us0", "us1",
             "dd0", "dd1", "p0", "p1", "c0", "c1", "xo0", "xo1",
             "yo0", "yo1", "rb0", "rb1", "c2", "c3")}

    s = Sched()
    wtok = {}
    rtok = {}

    def ps_half(which, half):
        t = psA if which == "A" else psB
        return t[:, half * 512:(half + 1) * 512]

    def mm(out_ap, lhsT, rhs, slot, start, stop, extra_waits=(),
           inc_anyway=False):
        if start:
            s.wait("tensor", rtok.get(slot))
        for t in extra_waits:
            s.wait("tensor", t)
        do_inc = stop or inc_anyway
        tok = s.op("tensor",
                   lambda E, o=out_ap, l=lhsT, r=rhs, st_=start, sp=stop:
                   E.matmul(o, l, r, start=st_, stop=sp),
                   inc="pe" if do_inc else None)
        if stop:
            wtok[slot] = tok
        return tok

    def read_slot(eng, slot, fn, inc, extra_waits=()):
        s.wait(eng, wtok.get(slot))
        for t in extra_waits:
            s.wait(eng, t)
        tok = s.op(eng, fn, inc=inc)
        rtok[slot] = tok
        return tok

    def dma_in(out_ap, in_ap, sem="ld", extra_waits=()):
        for t in extra_waits:
            s.wait("sync", t)
        return s.op("sync", lambda E, o=out_ap, i=in_ap: E.dma_start(out=o, in_=i),
                    inc=sem, n=16)

    def dma_out(out_ap, in_ap, sem="st", extra_waits=()):
        for t in extra_waits:
            s.wait("gpsimd", t)
        return s.op("gpsimd", lambda E, o=out_ap, i=in_ap: E.dma_start(out=o, in_=i),
                    inc=sem, n=16)

    dma_in(sm[:, :], smalls)
    dma_in(bv_sb[:, :], bv)
    t_sm = t_bv = ("ld", 32)

    s.op("gpsimd", lambda E: E.memset(ones_bf[:, :], 1.0), inc="pool")
    s.op("gpsimd", lambda E: E.memset(ones_f[:, :], 1.0), inc="pool")
    for v in range(4):
        s.op("gpsimd", lambda E, v=v: E.memset(masks[:, v, :], 1.0), inc="pool")
        s.wait("gpsimd", s.cur("pool"))
        s.op("gpsimd", lambda E, v=v: E.affine_select(
            out=masks[:, v, :], in_=masks[:, v, :], pattern=[[1, 512]],
            compare_op=ALU.is_ge, fill=0.0, base=-128 * v, channel_multiplier=-1),
            inc="pool")
    s.op("gpsimd", lambda E: E.memset(uw[0:1, 6:7], -0.5 * math.log(HD)),
         inc="pool")
    s.op("gpsimd", lambda E: E.memset(uw[0:1, 7:8], LN_EPS), inc="pool")
    t_const = s.cur("pool")
    s.wait("tensor", t_const)
    s.wait("vector", t_const)
    s.wait("scalar", t_const)

    s.wait("vector", t_bv)
    t_bvbf = s.op("vector", lambda E: E.tensor_copy(bv_bf[:, :], bv_sb[:, :]),
                  inc="dve")
    s.wait("vector", t_sm)
    t_dsos = s.op("vector", lambda E: E.tensor_scalar(
        dsos[:, :], sm[:, 88:104], sm[:, 104:105], None, ALU.mult), inc="dve")

    xstage_tok = [None, None]
    xblk_pe_tok = None

    wq_tok = [None] * 12
    wqs_tok = [None, None]

    def load_wq_piece(i):
        tl = dma_in(wst[i % 2][:, :], wq[i], sem=f"us{i % 2}",
                    extra_waits=[wqs_tok[i % 2]])
        s.wait("scalar", tl)
        tok = s.op("scalar",
                   lambda E, i=i: E.activation(
                       wq_bf[:, :, i * 128:(i + 1) * 128],
                       wst[i % 2][:, :].rearrange("p (dc f) -> p dc f", dc=16),
                       AF.Copy, bias=0.0, scale=1.0),
                   inc="act")
        wqs_tok[i % 2] = tok
        wq_tok[i] = tok

    load_wq_piece(0)

    qk_slots = [("A", 0), ("A", 1), ("B", 0), ("B", 1)]
    nslot = 0
    vslot = 0
    for tb in range(8):
        for sl in range(4):
            gslab = tb * 4 + sl
            tl = dma_in(xstage[:, sl % 2, :], xq[gslab], sem=f"x{sl % 2}",
                        extra_waits=[xstage_tok[sl % 2]])
            s.wait("vector", tl)
            if sl == 0:
                s.wait("vector", xblk_pe_tok)
            tok = s.op("vector",
                       lambda E, sl=sl: E.tensor_copy(
                           xblk[:, :, sl * 128:(sl + 1) * 128],
                           xstage[:, sl % 2, :].rearrange(
                               "p (dc f) -> p dc f", dc=16)),
                       inc="dve")
            xstage_tok[sl % 2] = tok
            if tb == 0:
                hi = 12 if sl == 3 else 2 + 3 * sl
                for i in range(1, hi):
                    if i < 12 and wq_tok[i] is None:
                        load_wq_piece(i)
        t_xblk = s.cur("dve")

        for et in range(8):
            slot = qk_slots[nslot % 4]
            nslot += 1
            for dc in range(16):
                mm(ps_half(*slot), wq_bf[:, dc, et * 128:(et + 1) * 128],
                   xblk[:, dc, :], slot, start=(dc == 0), stop=(dc == 15),
                   extra_waits=([wq_tok[et], t_xblk] if dc == 0 else []))
            xblk_pe_tok = wtok[slot]

            dst = Q_sb if et < 4 else K_sb
            h = et % 4
            col = tb * 512
            read_slot("scalar", slot,
                      lambda E, d=dst, h=h, c=col, sl_=slot, et=et:
                      E.activation(d[:, h, c:c + 512], ps_half(*sl_),
                                   AF.Identity, bias=sm[:, et:et + 1],
                                   scale=1.0),
                      inc="act", extra_waits=[t_sm])

        for tt in range(4):
            slot = ("O",) if vslot % 2 == 0 else ("M",)
            vslot += 1
            pst = psO if slot == ("O",) else psM
            for dc in range(16):
                mm(pst[:, :], xblk[:, dc, tt * 128:(tt + 1) * 128],
                   wq_bf[:, dc, 1024:1536], slot, start=(dc == 0), stop=False,
                   extra_waits=([t_xblk, wq_tok[8], wq_tok[9], wq_tok[10],
                                 wq_tok[11]] if dc == 0 else []))
            mm(pst[:, :], ones_bf[0:1, 0:128], bv_bf[0:1, :], slot,
               start=False, stop=True, extra_waits=[t_bvbf])
            xblk_pe_tok = wtok[slot]
            gt = tb * 4 + tt
            read_slot("scalar", slot,
                      lambda E, gt=gt, p=pst: E.activation(
                          V_sb[:, gt, :], p[:, :], AF.Copy, bias=0.0, scale=1.0),
                      inc="act")

    t_qkv_act = s.cur("act")
    rtok[("O", 0)] = rtok.get(("O",))
    rtok[("O", 1)] = rtok.get(("M",))
    rtok[("O", 2)] = None

    pt_tok = [None, None]
    u_read_tok = None
    c_read_tok = None
    pending_norm = []

    def emit_norm(h, qg, t_rec, t_oev):
        p = qg % 2
        d1 = dma_out(recd.ap()[p:p + 1, :], recs[p][0:1, :], sem=f"rb{p}",
                     extra_waits=[t_rec])
        rtok[("RECF", p)] = d1
        bsrc = bass.AP(recd, p * 512, [[0, 128], [1, 512]])
        d2 = dma_out(recbc[:, p, :], bsrc, sem=f"rb{p}",
                     extra_waits=[rtok.get(("RB", p)), d1])
        s.wait("vector", d2)
        s.wait("vector", t_oev)
        t_onorm = s.op("vector",
                       lambda E, h=h, qg=qg: E.tensor_mul(
                           O_sb[:, h, qg * 512:(qg + 1) * 512],
                           O_sb[:, h, qg * 512:(qg + 1) * 512],
                           recbc[:, qg % 2, :]),
                       inc="dve")
        rtok[("RB", p)] = t_onorm
        return t_onorm

    def flush_norms(upto):
        while len(pending_norm) > upto:
            emit_norm(*pending_norm.pop(0))

    for h in range(4):
        qsq_tok = None
        for side, src in ((0, Q_sb), (1, K_sb)):
            for ch in range(8):
                s.wait("vector", t_qkv_act)
                s.wait("vector", qsq_tok)
                sq_t = s.op("vector",
                            lambda E, src=src, h=h, ch=ch: E.tensor_mul(
                                qsq[:, :], src[:, h, ch * 512:(ch + 1) * 512],
                                src[:, h, ch * 512:(ch + 1) * 512]),
                            inc="dve")
                s.wait("tensor", u_read_tok)
                for p_ in range(3):
                    s.wait("tensor", rtok.get(("M", p_)))
                s.wait("tensor", sq_t)
                pe_t = s.op("tensor",
                            lambda E: E.matmul(psC[0:1, :], ones_bf[:, 0:1],
                                               qsq[:, :], start=True, stop=True),
                            inc="pe")
                qsq_tok = pe_t
                s.wait("vector", pe_t)
                rm_t = s.op("vector",
                            lambda E: E.reduce_max(uw[0:1, 4:5], psC[0:1, :],
                                                   axis=AX.X),
                            inc="dve")
                u_read_tok = rm_t
                s.wait("vector", rm_t)
                if ch == 0:
                    s.op("vector", lambda E, sd=side: E.tensor_copy(
                        uw[0:1, sd:sd + 1], uw[0:1, 4:5]), inc="dve")
                else:
                    s.op("vector", lambda E, sd=side: E.tensor_max(
                        uw[0:1, sd:sd + 1], uw[0:1, sd:sd + 1], uw[0:1, 4:5]),
                        inc="dve")
        s.wait("vector", s.cur("dve"))
        t_prod = s.op("vector", lambda E: E.tensor_mul(
            uw[0:1, 2:3], uw[0:1, 0:1], uw[0:1, 1:2]), inc="dve")
        s.wait("scalar", t_prod)
        t_ln = s.op("scalar", lambda E: E.activation(
            uw[0:1, 3:4], uw[0:1, 2:3], AF.Ln), inc="act")
        s.wait("scalar", t_ln)
        t_u = s.op("scalar", lambda E: E.activation(
            uw[0:1, 5:6], uw[0:1, 3:4], AF.Exp, bias=uw[0:1, 6:7],
            scale=0.5), inc="act")
        for p_ in range(3):
            rtok[("M", p_)] = u_read_tok
        s.wait("vector", t_u)
        t_nub = s.op("vector", lambda E: E.tensor_scalar(
            nub[0:1, 0:1], uw[0:1, 5:6], -1.0, None, ALU.mult), inc="dve")
        s.wait("tensor", t_nub)
        for p_ in range(3):
            s.wait("tensor", rtok.get(("M", p_)))
            s.wait("tensor", rtok.get(("O", p_)))
        t_bc = s.op("tensor", lambda E: E.matmul(
            psC[:, 0:1], ones_bf[0:1, :], nub[0:1, 0:1], start=True, stop=True),
            inc="pe")
        s.wait("scalar", t_bc)
        t_negu = s.op("scalar", lambda E: E.activation(
            negU[:, 0:1], psC[:, 0:1], AF.Copy, bias=0.0, scale=1.0), inc="act")
        s.wait("scalar", t_negu)
        for p_ in range(3):
            rtok[("M", p_)] = t_negu

        for qg in range(8):
            npairs = 2 * qg + 2
            exp_toks = {}
            mask_toks = {}
            flush_norms(1)
            for pr in range(npairs + 1):
                if pr < npairs:
                    pst = psA if pr % 2 == 0 else psB
                    slot = ("S", pr % 2)
                    for half in range(2):
                        kt = 2 * pr + half
                        mm(pst[:, half * 512:(half + 1) * 512],
                           K_sb[:, h, kt * 128:(kt + 1) * 128],
                           Q_sb[:, h, qg * 512:(qg + 1) * 512],
                           slot, start=True, stop=True,
                           extra_waits=[t_qkv_act] if half == 0 else [])
                    read_slot("scalar", slot,
                              lambda E, pst=pst, pr=pr: E.activation(
                                  PT[:, pr % 2, :], pst[:, :], AF.Exp,
                                  bias=negU[:, 0:1], scale=INV_SQRT_HD),
                              inc="act", extra_waits=[pt_tok[pr % 2]])
                    exp_toks[pr] = s.cur("act")
                    for half in range(2):
                        kt = 2 * pr + half
                        v = kt - 4 * qg
                        if 0 <= v <= 3:
                            s.wait("vector", exp_toks[pr])
                            mt = s.op("vector",
                                      lambda E, pr=pr, half=half, v=v:
                                      E.tensor_mul(
                                          PT[:, pr % 2,
                                             half * 512:(half + 1) * 512],
                                          PT[:, pr % 2,
                                             half * 512:(half + 1) * 512],
                                          masks[:, v, :]),
                                      inc="dve")
                            mask_toks[(pr, half)] = mt
                if pr > 0:
                    ppr = pr - 1
                    last = (ppr == npairs - 1)
                    po = [psO, psM, psU][qg % 3]
                    oslot = ("O", qg % 3)
                    mslot = ("M", qg % 3)
                    mrow = 32 * (qg % 3)
                    for half in range(2):
                        kt = 2 * ppr + half
                        ew = [exp_toks[ppr]]
                        if (ppr, half) in mask_toks:
                            ew.append(mask_toks[(ppr, half)])
                        mm(po[:, :], V_sb[:, kt, h * 128:(h + 1) * 128],
                           PT[:, ppr % 2, half * 512:(half + 1) * 512],
                           oslot, start=(kt == 0), stop=(last and half == 1),
                           extra_waits=ew)
                        mm(psC[mrow:mrow + 1, :], ones_bf[:, 0:1],
                           PT[:, ppr % 2, half * 512:(half + 1) * 512],
                           mslot, start=(kt == 0), stop=(last and half == 1),
                           extra_waits=[], inc_anyway=(half == 1))
                    pt_tok[ppr % 2] = s.cur("pe")
            po = [psO, psM, psU][qg % 3]
            oslot = ("O", qg % 3)
            mslot = ("M", qg % 3)
            mrow = 32 * (qg % 3)
            s.wait("vector", wtok[mslot])
            s.wait("vector", rtok.get(("RECF", qg % 2)))
            t_rec = s.op("vector", lambda E, qg=qg, mrow=mrow: E.reciprocal(
                recs[qg % 2][0:1, :], psC[mrow:mrow + 1, :]),
                inc="dve")
            rtok[mslot] = t_rec
            t_oev = read_slot("scalar", oslot,
                              lambda E, h=h, qg=qg, po=po: E.activation(
                                  O_sb[:, h, qg * 512:(qg + 1) * 512],
                                  po[:, :], AF.Copy, bias=0.0, scale=1.0),
                              inc="act")
            pending_norm.append((h, qg, t_rec, t_oev))

    flush_norms(0)
    t_attn_dve = s.cur("dve")

    wout_tok = None
    for dc in range(4):
        tl = dma_in(xstage[:, dc % 2, :], wo[dc], sem=f"x{dc % 2}",
                    extra_waits=[xstage_tok[dc % 2]])
        s.wait("vector", tl)
        ct = s.op("vector",
                  lambda E, dc=dc: E.tensor_copy(wout_bf[:, dc, :],
                                                 xstage[:, dc % 2, :]),
                  inc="dve")
        xstage_tok[dc % 2] = ct
        wout_tok = ct

    cs_tok = [None] * 4
    op_slots = [("A", 0), ("A", 1), ("B", 0), ("B", 1)]
    ns = 0
    cseq = 0
    t_cc = {}
    for idx, tcg in enumerate([0, 2, 4, 6, 1, 3, 5, 7]):
        half = tcg % 2
        for et in range(16):
            slot = op_slots[ns % 4]
            for dc in range(4):
                mm(ps_half(*slot), wout_bf[:, dc, et * 128:(et + 1) * 128],
                   O_sb[:, dc, tcg * 512:(tcg + 1) * 512], slot,
                   start=(dc == 0), stop=(dc == 3),
                   extra_waits=([wout_tok, t_attn_dve] if dc == 0 else []))
            sl = cseq % 4
            ns += 1
            cseq += 1
            if et % 2 == 0:
                ev = read_slot("scalar", slot,
                               lambda E, sl=sl, s_=slot: E.activation(
                                   cstage[:, sl, :], ps_half(*s_), AF.Copy,
                                   bias=0.0, scale=1.0),
                               inc="act", extra_waits=[cs_tok[sl]])
            else:
                ev = read_slot("vector", slot,
                               lambda E, sl=sl, s_=slot: E.tensor_copy(
                                   cstage[:, sl, :], ps_half(*s_)),
                               inc="dve", extra_waits=[cs_tok[sl]])
            r0 = (tcg // 2) * 2048 + et * 128
            dst = cbuf2[half].ap()[r0:r0 + 128, :]
            if sl % 2 == 0:
                cs_tok[sl] = dma_out(dst, cstage[:, sl, :], sem=f"c{sl}",
                                     extra_waits=[ev])
            else:
                cs_tok[sl] = dma_in(dst, cstage[:, sl, :], sem=f"c{sl}",
                                    extra_waits=[ev])
        if idx == 3 or idx == 7:
            h_ = 0 if idx == 3 else 1
            for cn in ("c0", "c1", "c2", "c3"):
                s.wait("gpsimd", s.cur(cn))
            t_cc[h_] = s.op("gpsimd", lambda E, h_=h_: E.collective_compute(
                "ReduceScatter", ALU.add,
                replica_groups=[[0, 1, 2, 3], [4, 5, 6, 7]],
                ins=[cbuf2[h_].ap().opt()], outs=[rbuf2[h_].ap().opt()]),
                inc="cc")
        if idx == 3:
            t_op_pe_evens = s.cur("pe")
    t_op_pe = s.cur("pe")

    up_pref = []
    for i in range(2):
        up_pref.append(dma_in(ustage[:, i, :], up[i], sem=f"us{i}",
                              extra_waits=[t_op_pe]))
    s.wait("gpsimd", t_cc[1])

    def layer_norm(src_big, rb, sq, tt1, mbc, rbc, out_cb, start_tok=None,
                   src_toks=None):
        slots_ps = [psO, psM, psC, psU]
        rb_tok = [start_tok, start_tok]
        for et in range(16):
            sl = et % 2
            s.wait("vector", rb_tok[sl])
            s.wait("vector", rb_tok[1 - sl])
            s.wait("scalar", rb_tok[sl])
            s.wait("scalar", rb_tok[1 - sl])
            if src_toks is not None:
                s.wait("scalar", src_toks[et])
            tcast = s.op("scalar",
                         lambda E, et=et, sl=sl, rb=rb, src=src_big:
                         E.activation(rb[:, sl, :], src[:, et, :], AF.Copy,
                                      bias=0.0, scale=1.0), inc="act")
            s.wait("vector", tcast)
            tsq = s.op("vector",
                       lambda E, sl=sl, rb=rb, sq=sq: E.tensor_mul(
                           sq[:, :], rb[:, sl, :], rb[:, sl, :]), inc="dve")
            for half in range(2):
                mm(slots_ps[half][0:1, :], ones_bf[:, 0:1],
                   rb[:, sl, half * 512:(half + 1) * 512], ("LNM", half),
                   start=(et == 0), stop=(et == 15), extra_waits=[tcast],
                   inc_anyway=True)
            for half in range(2):
                mm(slots_ps[2 + half][0:1, :], ones_bf[:, 0:1],
                   sq[:, half * 512:(half + 1) * 512], ("LNV", half),
                   start=(et == 0), stop=(et == 15), extra_waits=[tsq],
                   inc_anyway=True)
            rb_tok[sl] = s.cur("pe")
        t_pe_rows = s.cur("pe")
        s.wait("vector", t_pe_rows)
        for half in range(2):
            s.op("vector", lambda E, half=half: E.tensor_scalar(
                rows[0:1, half * 512:half * 512 + 512],
                slots_ps[half][0:1, :], 1.0 / D, None, ALU.mult), inc="dve")
            s.op("vector", lambda E, half=half: E.tensor_scalar(
                rows[0:1, 1024 + half * 512:1024 + half * 512 + 512],
                slots_ps[2 + half][0:1, :], 1.0 / D, None, ALU.mult), inc="dve")
        s.wait("vector", s.cur("dve"))
        s.op("vector", lambda E: E.tensor_mul(
            rows[0:1, 2048:3072], rows[0:1, 0:1024], rows[0:1, 0:1024]),
            inc="dve")
        s.wait("vector", s.cur("dve"))
        t_var = s.op("vector", lambda E: E.tensor_sub(
            rows[0:1, 1024:2048], rows[0:1, 1024:2048], rows[0:1, 2048:3072]),
            inc="dve")
        s.wait("vector", t_var)
        t_var = s.op("vector", lambda E: E.tensor_scalar(
            rows[0:1, 1024:2048], rows[0:1, 1024:2048], LN_EPS, None, ALU.add),
            inc="dve")
        for key in (("LNM", 0), ("LNM", 1), ("LNV", 0), ("LNV", 1)):
            rtok[key] = t_var
        s.wait("scalar", t_var)
        t_lnv = s.op("scalar", lambda E: E.activation(
            rows[0:1, 2048:3072], rows[0:1, 1024:2048], AF.Ln), inc="act")
        s.wait("scalar", t_lnv)
        t_rstd = s.op("scalar", lambda E: E.activation(
            rows[0:1, 3072:4096], rows[0:1, 2048:3072], AF.Exp, scale=-0.5),
            inc="act")
        s.wait("vector", t_rstd)
        s.op("vector", lambda E: E.tensor_mul(
            rows[0:1, 2048:3072], rows[0:1, 3072:4096], rows[0:1, 3072:4096]),
            inc="dve")
        s.wait("vector", s.cur("dve"))
        s.op("vector", lambda E: E.tensor_mul(
            rows[0:1, 2048:3072], rows[0:1, 2048:3072], rows[0:1, 1024:2048]),
            inc="dve")
        s.wait("vector", s.cur("dve"))
        s.op("vector", lambda E: E.tensor_scalar(
            rows[0:1, 2048:3072], rows[0:1, 2048:3072], -0.5, 1.5,
            ALU.mult, ALU.add), inc="dve")
        s.wait("vector", s.cur("dve"))
        t_rstd = s.op("vector", lambda E: E.tensor_mul(
            rows[0:1, 3072:4096], rows[0:1, 3072:4096], rows[0:1, 2048:3072]),
            inc="dve")
        t_bc_done = None
        for half in range(2):
            for ri, dst in ((0, mbc), (3, rbc)):
                src_row = rows[0:1,
                               ri * 1024 + half * 512:ri * 1024 + half * 512 + 512]
                s.wait("tensor", t_rstd if ri == 3 else t_var)
                s.wait("tensor", rtok.get(("LNBC",)))
                tbc = s.op("tensor",
                           lambda E, sr=src_row: E.matmul(
                               psC[:, :], ones_f[0:1, :], sr, start=True,
                               stop=True), inc="pe")
                s.wait("scalar", tbc)
                tcp = s.op("scalar",
                           lambda E, dst=dst, half=half: E.activation(
                               dst[:, half * 512:(half + 1) * 512], psC[:, :],
                               AF.Copy, bias=0.0, scale=1.0), inc="act")
                rtok[("LNBC",)] = tcp
                t_bc_done = tcp
        for et in range(16):
            s.wait("vector", t_bc_done)
            tsub = s.op("vector", lambda E, et=et, tt1=tt1, src=src_big, mbc=mbc:
                        E.tensor_sub(tt1[:, :], src[:, et, :], mbc[:, :]),
                        inc="dve")
            s.wait("vector", tsub)
            out_cb(et)
            s.wait("vector", s.cur("dve"))
        return t_bc_done

    xpl_tok = [None, None]
    xln_done = [None] * 2

    r_tok = [None, None]
    xr_tok = [None, None]
    rb_tok = [None, None]

    def ln_half(half, t_ready_tok):
        c0 = half * 512
        for et in range(16):
            sl = et % 2
            ew = [r_tok[sl]]
            if et < 2:
                ew.append(t_ready_tok)
            tl1 = dma_in(rstage[:, sl, 0:512],
                         rbuf2[half].ap()[et * 128:(et + 1) * 128, :],
                         sem=f"r{sl}", extra_waits=ew)
            tl2 = dma_in(xrstage[:, sl, :], xres[et][:, c0:c0 + 512],
                         sem=f"xr{sl}", extra_waits=[xr_tok[sl]])
            s.wait("vector", tl1)
            s.wait("vector", tl2)
            s.wait("vector", t_sm)
            tres = s.op("vector",
                        lambda E, et=et, sl=sl, c0=c0: E.scalar_tensor_tensor(
                            bigf[:, et, c0:c0 + 512], rstage[:, sl, 0:512],
                            sm[:, 8 + et:9 + et], xrstage[:, sl, :],
                            ALU.add, ALU.add),
                        inc="dve")
            r_tok[sl] = tres
            xr_tok[sl] = tres
            s.wait("scalar", tres)
            s.wait("scalar", rb_tok[sl])
            s.wait("scalar", rb_tok[1 - sl])
            tcast = s.op("scalar",
                         lambda E, et=et, sl=sl, c0=c0: E.activation(
                             rb16[:, sl, 0:512], bigf[:, et, c0:c0 + 512],
                             AF.Copy, bias=0.0, scale=1.0), inc="act")
            s.wait("vector", tcast)
            tsq = s.op("vector",
                       lambda E, sl=sl: E.tensor_mul(
                           sqb[:, 0:512], rb16[:, sl, 0:512],
                           rb16[:, sl, 0:512]), inc="dve")
            mm(psO[0:1, :], ones_bf[:, 0:1], rb16[:, sl, 0:512],
               ("LNM", half), start=(et == 0), stop=(et == 15),
               extra_waits=[tcast], inc_anyway=True)
            mm(psM[0:1, :], ones_bf[:, 0:1], sqb[:, 0:512],
               ("LNV", half), start=(et == 0), stop=(et == 15),
               extra_waits=[tsq], inc_anyway=True)
            rb_tok[sl] = s.cur("pe")
        t_pe_rows = s.cur("pe")
        s.wait("vector", t_pe_rows)
        R = rows[0:1, :]
        mrow = rows[0:1, c0:c0 + 512]
        vrow = rows[0:1, 1024 + c0:1024 + c0 + 512]
        srow = rows[0:1, 2048 + c0:2048 + c0 + 512]
        rrow = rows[0:1, 3072 + c0:3072 + c0 + 512]
        s.op("vector", lambda E: E.tensor_scalar(
            mrow, psO[0:1, :], 1.0 / D, None, ALU.mult), inc="dve")
        s.op("vector", lambda E: E.tensor_scalar(
            vrow, psM[0:1, :], 1.0 / D, None, ALU.mult), inc="dve")
        s.wait("vector", s.cur("dve"))
        s.op("vector", lambda E: E.tensor_mul(srow, mrow, mrow), inc="dve")
        s.wait("vector", s.cur("dve"))
        s.op("vector", lambda E: E.tensor_sub(vrow, vrow, srow), inc="dve")
        s.wait("vector", s.cur("dve"))
        t_var = s.op("vector", lambda E: E.tensor_scalar(
            vrow, vrow, LN_EPS, None, ALU.add), inc="dve")
        rtok[("LNM", half)] = t_var
        rtok[("LNV", half)] = t_var
        s.wait("scalar", t_var)
        t_lnv = s.op("scalar", lambda E: E.activation(srow, vrow, AF.Ln),
                     inc="act")
        s.wait("scalar", t_lnv)
        t_rstd = s.op("scalar", lambda E: E.activation(
            rrow, srow, AF.Exp, scale=-0.5), inc="act")
        s.wait("vector", t_rstd)
        s.op("vector", lambda E: E.tensor_mul(srow, rrow, rrow), inc="dve")
        s.wait("vector", s.cur("dve"))
        s.op("vector", lambda E: E.tensor_mul(srow, srow, vrow), inc="dve")
        s.wait("vector", s.cur("dve"))
        s.op("vector", lambda E: E.tensor_scalar(
            srow, srow, -0.5, 1.5, ALU.mult, ALU.add), inc="dve")
        s.wait("vector", s.cur("dve"))
        t_rstd = s.op("vector", lambda E: E.tensor_mul(rrow, rrow, srow),
                      inc="dve")
        s.wait("tensor", t_var)
        s.wait("tensor", rtok.get(("LNBC",)))
        tb1 = s.op("tensor", lambda E: E.matmul(
            psC[:, :], ones_f[0:1, :], mrow, start=True, stop=True), inc="pe")
        s.wait("scalar", tb1)
        cm1 = s.op("scalar", lambda E: E.activation(
            meanbc[:, 0:512], psC[:, :], AF.Copy, bias=0.0, scale=1.0),
            inc="act")
        s.wait("tensor", t_rstd)
        s.wait("tensor", rtok.get(("LNBC2",)))
        tb2 = s.op("tensor", lambda E: E.matmul(
            psU[:, :], ones_f[0:1, :], rrow, start=True, stop=True), inc="pe")
        s.wait("scalar", tb2)
        cm2 = s.op("scalar", lambda E: E.activation(
            rstdbc[:, 0:512], psU[:, :], AF.Copy, bias=0.0, scale=1.0),
            inc="act")
        rtok[("LNBC",)] = cm1
        rtok[("LNBC2",)] = cm2
        for et in range(16):
            sl = et % 2
            s.wait("vector", cm1)
            s.wait("vector", cm2)
            tsub = s.op("vector",
                        lambda E, et=et, c0=c0: E.tensor_sub(
                            t1[:, 0:512], bigf[:, et, c0:c0 + 512],
                            meanbc[:, 0:512]), inc="dve")
            s.wait("vector", tsub)
            tk1 = s.op("vector", lambda E, et=et, c0=c0: E.tensor_mul(
                xln[:, et, c0:c0 + 512], t1[:, 0:512], rstdbc[:, 0:512]),
                inc="dve")
            s.wait("vector", xpl_tok[sl])
            tk2 = s.op("vector", lambda E, sl=sl, c0=c0: E.tensor_mul(
                xplstage[:, sl, 0:512], t1[:, 0:512], rstdbc[:, 0:512]),
                inc="dve")
            s.wait("vector", s.cur("dve"))
            xpl_tok[sl] = dma_out(xln1d[et][:, c0:c0 + 512],
                                  xplstage[:, sl, 0:512], sem=f"xo{sl}",
                                  extra_waits=[tk2])
        xln_done[half] = s.cur("dve")

    ln_half(0, t_cc[0])
    ln_half(1, t_cc[1])
    t_ln1 = s.cur("dve")
    t_xpl_w0 = s.cur("xo0")
    t_xpl_w1 = s.cur("xo1")

    us_tok = [None, None]
    ub_read = [None, None]
    sign_tok = [None, None]
    ds_tok = [None, None]
    dn_read = [None] * 8
    hg_read = [None] * 8
    acc_tok = [[None, None] for _ in range(16)]
    hd_slots = [("A", 0), ("A", 1)]
    dd_slots = [("B", 0), ("B", 1)]

    for g in range(8):
        for hci in range(8):
            ht = g * 8 + hci
            usl = ht % 2
            if ht < 2:
                tl = up_pref[ht]
            else:
                tl = dma_in(ustage[:, usl, :], up[ht], sem=f"us{usl}",
                            extra_waits=[sign_tok[usl]])
            s.wait("scalar", tl)
            s.wait("scalar", ub_read[usl])
            tsg = s.op("scalar",
                       lambda E, usl=usl: E.activation(
                           ub16[:, usl, :], ustage[:, usl, :], AF.Sign),
                       inc="act")
            sign_tok[usl] = tsg
            for tc in range(2):
                for dc in range(16):
                    ew = []
                    if dc == 0:
                        ew = [tsg, xln_done[tc]]
                    mm(ps_half(*hd_slots[tc]),
                       ub16[:, usl, dc * 128:(dc + 1) * 128],
                       xln[:, dc, tc * 512:(tc + 1) * 512], hd_slots[tc],
                       start=(dc == 0), stop=(dc == 15), extra_waits=ew)
            ub_read[usl] = wtok[hd_slots[1]]
            for tc in range(2):
                read_slot("vector", hd_slots[tc],
                          lambda E, hci=hci, tc=tc, ht=ht, s_=hd_slots[tc]:
                          E.tensor_scalar(
                              hgrp[:, hci, tc * 512:(tc + 1) * 512],
                              ps_half(*s_), sm[:, 24 + ht:25 + ht], 0.0,
                              ALU.mult, ALU.max),
                          inc="dve", extra_waits=[hg_read[hci], t_sm])
        t_relu_g = s.cur("dve")

        for hci in range(8):
            dsl = hci % 2
            ew = [ds_tok[dsl]]
            if g == 0 and hci < 2:
                ew += [t_ln1, t_xpl_w0, t_xpl_w1]
            tl = dma_in(dstage[:, dsl, :], down[g, hci], sem=f"dd{dsl}",
                        extra_waits=ew)
            s.wait("scalar", tl)
            s.wait("scalar", dn_read[hci])
            if g == 0 and hci < 2:
                s.wait("scalar", t_xpl_w0)
                s.wait("scalar", t_xpl_w1)
                s.wait("scalar", t_ln1)
            tsg = s.op("scalar",
                       lambda E, hci=hci, dsl=dsl: E.activation(
                           dn16[:, hci, :, :].rearrange("p a b -> p (a b)"),
                           dstage[:, dsl, :], AF.Sign),
                       inc="act")
            ds_tok[dsl] = tsg
        t_sign_g = s.cur("act")

        for et in range(16):
            for tc in range(2):
                for hci in range(8):
                    ew = []
                    if hci == 0:
                        ew = [t_sign_g, t_relu_g]
                    mm(ps_half(*dd_slots[tc]), dn16[:, hci, et, :],
                       hgrp[:, hci, tc * 512:(tc + 1) * 512], dd_slots[tc],
                       start=(hci == 0), stop=(hci == 7), extra_waits=ew)
            for tc in range(2):
                if g == 0:
                    fn = (lambda E, et=et, tc=tc, s_=dd_slots[tc]:
                          E.tensor_copy(bigf[:, et, tc * 512:(tc + 1) * 512],
                                        ps_half(*s_)))
                else:
                    fn = (lambda E, et=et, tc=tc, s_=dd_slots[tc]:
                          E.tensor_add(bigf[:, et, tc * 512:(tc + 1) * 512],
                                       bigf[:, et, tc * 512:(tc + 1) * 512],
                                       ps_half(*s_)))
                read_slot("vector", dd_slots[tc], fn, inc="dve")
                acc_tok[et][tc] = s.cur("dve")
        t_down_g = s.cur("pe")
        for hci in range(8):
            hg_read[hci] = t_down_g
            dn_read[hci] = t_down_g
    t_down_pe = s.cur("pe")
    t_sign_all = s.cur("act")

    x2_tok = [None, None]
    res2_tok = [None] * 16
    for et in range(16):
        sl = et % 2
        ew = [x2_tok[sl]]
        if et < 2:
            ew += [t_down_pe, t_sign_all, t_xpl_w0, t_xpl_w1]
        tl = dma_in(xpl2[:, sl, :], xln1d[et], sem=f"p{sl}", extra_waits=ew)
        s.wait("vector", tl)
        s.wait("vector", t_dsos)
        tres = s.op("vector",
                    lambda E, et=et, sl=sl: E.scalar_tensor_tensor(
                        bigf[:, et, :], bigf[:, et, :], dsos[:, et:et + 1],
                        xpl2[:, sl, :], ALU.mult, ALU.add),
                    inc="dve")
        x2_tok[sl] = tres
        res2_tok[et] = tres

    f_tok = [None, None]
    y_writes = []

    def ln2_out(et):
        sl = et % 2
        s.wait("vector", f_tok[sl])
        tk = s.op("vector", lambda E, sl=sl: E.tensor_mul(
            fstage[:, sl, :], t1b[:, :], rstdbc2[:, :]), inc="dve")
        dt = dma_out(y[et], fstage[:, sl, :], sem=f"yo{sl}",
                     extra_waits=[tk])
        f_tok[sl] = dt
        y_writes.append(dt)

    layer_norm(bigf, rb16b, sqb2, t1b, meanbc2, rstdbc2, ln2_out,
               start_tok=t_down_pe, src_toks=res2_tok)

    s.wait("gpsimd", s.cur("yo0"))
    s.wait("gpsimd", s.cur("yo1"))

    with nc.Block() as block:
        def runner(eng_name):
            def go(E):
                for item in s.ops[eng_name]:
                    if item[0] == "w":
                        _, sname, v = item
                        E.wait_ge(sems[sname], v)
                    else:
                        _, fn, inc, n = item
                        ins = fn(E)
                        if inc is not None:
                            ins.then_inc(sems[inc], n)
            return go

        block.sync(runner("sync"))
        block.gpsimd(runner("gpsimd"))
        block.vector(runner("vector"))
        block.scalar(runner("scalar"))
        block.tensor(runner("tensor"))

    return nc


def _prep_inputs(L0_attn_x, L0_attn_w_qkv, L0_attn_b_qkv, L0_attn_w_out,
                 L0_attn_b_out, L0_ffn_up_weight, L0_ffn_down_weight,
                 L0_ffn_up_scale, L0_ffn_down_scale, L0_ffn_output_scale):
    x = np.asarray(L0_attn_x, np.float32)
    wqkv = np.asarray(L0_attn_w_qkv, np.float32)
    bqkv = np.asarray(L0_attn_b_qkv, np.float32)
    wout = np.asarray(L0_attn_w_out, np.float32)
    bout = np.asarray(L0_attn_b_out, np.float32)
    upw = np.asarray(L0_ffn_up_weight, np.float32)
    dnw = np.asarray(L0_ffn_down_weight, np.float32)
    usc = np.asarray(L0_ffn_up_scale, np.float32)
    dsc = np.asarray(L0_ffn_down_scale, np.float32)
    osc = np.asarray(L0_ffn_output_scale, np.float32).reshape(1)

    up_p = np.ascontiguousarray(
        upw.reshape(64, 128, 16, 128).transpose(0, 3, 2, 1)).reshape(
        64, 128, 2048)
    down_p = np.ascontiguousarray(dnw.T).reshape(8, 8, 128, 2048)
    woutT = np.ascontiguousarray(wout.T)

    in_maps = []
    xq_cache = {}
    xt_cache = {}
    for c in range(8):
        b, j = c // 4, c % 4
        if b not in xq_cache:
            xq_cache[b] = np.ascontiguousarray(
                x[b].reshape(32, 128, 16, 128).transpose(0, 3, 2, 1)
            ).reshape(32, 128, 2048)
            xt_cache[b] = np.ascontiguousarray(x[b].T)
        rows = []
        for sec in range(3):
            base = sec * 2048 + j * 512
            rows.append(np.arange(base, base + 512))
        rows_idx = np.concatenate(rows)
        wsl = wqkv[rows_idx]
        wq_p = np.ascontiguousarray(
            wsl.reshape(12, 128, 16, 128).transpose(0, 3, 2, 1)).reshape(
            12, 128, 2048)
        bqk = bqkv[rows_idx[:1024]].reshape(8, 128).T
        bv_r = np.ascontiguousarray(bqkv[rows_idx[1024:]].reshape(1, 512))
        wout_p = np.ascontiguousarray(woutT[j * 512:(j + 1) * 512]).reshape(
            4, 128, 2048)
        tok0 = j * 1024
        xres_p = np.ascontiguousarray(
            xt_cache[b][:, tok0:tok0 + 1024]).reshape(16, 128, 1024)
        sm_a = np.zeros((128, 112), np.float32)
        sm_a[:, 0:8] = bqk
        sm_a[:, 8:24] = bout.reshape(16, 128).T
        sm_a[:, 24:88] = usc.reshape(64, 128).T
        sm_a[:, 88:104] = dsc.reshape(16, 128).T
        sm_a[:, 104] = osc[0]
        in_maps.append({
            "xq": xq_cache[b], "wq": wq_p, "wo": wout_p, "xres": xres_p,
            "up": up_p, "down": down_p, "smalls": sm_a, "bv": bv_r,
        })
    return in_maps


_NC_CACHE = {}


def kernel(**inputs):
    in_maps = _prep_inputs(**inputs)
    if "nc" not in _NC_CACHE:
        _NC_CACHE["nc"] = build_program()
    nc = _NC_CACHE["nc"]
    res = run_bass_kernel_spmd(nc, in_maps, core_ids=list(range(8)))
    out = np.empty((2, 4096, 2048), np.float32)
    for c in range(8):
        b, j = c // 4, c % 4
        yc = np.asarray(res.results[c]["y"])
        out[b, j * 1024:(j + 1) * 1024, :] = yc.transpose(2, 0, 1).reshape(
            1024, 2048)
    return out


# version 17
# speedup vs baseline: 1.0832x; 1.0004x over previous
import math
import numpy as np

import concourse.bass as bass
import concourse.mybir as mybir
from concourse.bass_utils import run_bass_kernel_spmd

F32 = mybir.dt.float32
BF16 = mybir.dt.bfloat16
AF = mybir.ActivationFunctionType
ALU = mybir.AluOpType
AX = mybir.AxisListType

S = 4096
D = 2048
HD = 128
INV_SQRT_HD = 1.0 / math.sqrt(HD)
LN_EPS = 1e-5

ENGS = ("sync", "gpsimd", "vector", "scalar", "tensor")
PART_BYTES = 229344


class Sched:

    def __init__(self):
        self.ops = {e: [] for e in ENGS}
        self.cnt = {}
        self.waited = {e: {} for e in ENGS}

    def wait(self, eng, tok):
        if tok is None:
            return
        sname, v = tok
        if v <= 0 or self.waited[eng].get(sname, 0) >= v:
            return
        self.waited[eng][sname] = v
        self.ops[eng].append(("w", sname, v))

    def op(self, eng, fn, inc=None, n=1):
        tok = None
        if inc is not None:
            self.cnt[inc] = self.cnt.get(inc, 0) + n
            tok = (inc, self.cnt[inc])
        self.ops[eng].append(("o", fn, inc, n))
        return tok

    def cur(self, sem):
        return (sem, self.cnt.get(sem, 0))


def build_program():
    nc = bass.Bass("TRN2", target_bir_lowering=False, debug=False)

    xq = nc.dram_tensor("xq", [32, 128, 2048], F32, kind="ExternalInput").ap()
    wq = nc.dram_tensor("wq", [12, 128, 2048], F32, kind="ExternalInput").ap()
    wo = nc.dram_tensor("wo", [4, 128, 2048], F32, kind="ExternalInput").ap()
    xres = nc.dram_tensor("xres", [16, 128, 1024], F32, kind="ExternalInput").ap()
    up = nc.dram_tensor("up", [64, 128, 2048], F32, kind="ExternalInput").ap()
    down = nc.dram_tensor("down", [8, 8, 128, 2048], F32,
                          kind="ExternalInput").ap()
    smalls = nc.dram_tensor("smalls", [128, 112], F32, kind="ExternalInput").ap()
    bv = nc.dram_tensor("bv", [1, 512], F32, kind="ExternalInput").ap()
    y = nc.dram_tensor("y", [16, 128, 1024], F32, kind="ExternalOutput").ap()

    cbuf2 = [nc.dram_tensor("cbufa", [8192, 512], BF16),
             nc.dram_tensor("cbufb", [8192, 512], BF16)]
    rbuf2 = [nc.dram_tensor("rbufa", [2048, 512], BF16),
             nc.dram_tensor("rbufb", [2048, 512], BF16)]
    xln1d = nc.dram_tensor("xln1d", [16, 128, 1024], F32)
    recd = nc.dram_tensor("recd", [2, 512], F32)

    def sb(name, shape, dtype, off):
        per = int(np.prod(shape[1:])) * mybir.dt.size(dtype)
        assert off + per <= PART_BYTES, f"{name} overflows: {off}+{per}"
        return nc.alloc_sbuf_tensor_at(name, shape, dtype, offset=off)

    ones_bf = sb("ones_bf", [128, 128], BF16, 16512)
    ones_f = sb("ones_f", [1, 128], F32, 16768)
    masks = sb("masks", [128, 4, 512], BF16, 17280)
    sm = sb("sm", [128, 112], F32, 21376)
    dsos = sb("dsos", [128, 16], F32, 21824)
    bv_sb = sb("bv_sb", [1, 512], F32, 21888)
    bv_bf = sb("bv_bf", [1, 512], BF16, 23936)
    negU = sb("negU", [128, 1], F32, 24960)
    uw = sb("uw", [1, 16], F32, 24992)
    nub = sb("nub", [1, 4], BF16, 25056)
    rec_f = sb("rec_f", [1, 512], F32, 25088)
    rec_f1 = sb("rec_f1", [1, 512], F32, 21888)
    recs = [rec_f, rec_f1]
    recbc = sb("recbc", [128, 2, 512], BF16, 27136)
    qsq = sb("qsq", [128, 512], BF16, 29184)
    Q_sb = sb("Q_sb", [128, 4, 4096], BF16, 30208)
    K_sb = sb("K_sb", [128, 4, 4096], BF16, 62976)
    V_sb = sb("V_sb", [128, 32, 512], BF16, 95744)
    wq_bf = sb("wq_bf", [128, 16, 1536], BF16, 128512)
    xstage = sb("xstage", [128, 2, 2048], F32, 177664)
    xblk = sb("xblk", [128, 16, 512], BF16, 194048)
    O_sb = sb("O_sb", [128, 4, 4096], BF16, 128512)
    PT = sb("PT", [128, 2, 1024], BF16, 161280)
    cstage = sb("cstage", [128, 4, 512], BF16, 165376)
    wout_bf = sb("wout_bf", [128, 4, 2048], BF16, 194048)
    bigf = sb("bigf", [128, 16, 1024], F32, 30208)
    xln = sb("xln", [128, 16, 1024], BF16, 95744)
    ustage = sb("ustage", [128, 2, 2048], F32, 128512)
    wst = [sb("wst_a", [128, 2048], F32, 212480),
           sb("wst_b", [128, 2048], F32, 220672)]
    qsqr = [sb(f"qsq{i}", [128, 512], BF16, 212480 + 1024 * i)
            for i in range(3)]
    negu4 = sb("negu4", [128, 4], F32, 215552)
    ub16 = sb("ub16", [128, 2, 2048], BF16, 144896)
    rows = sb("rows", [1, 4096], F32, 212480)
    rstage = sb("rstage", [128, 2, 1024], BF16, 153088)
    xrstage = sb("xrstage", [128, 2, 512], F32, 157184)
    rb16 = sb("rb16", [128, 2, 1024], BF16, 169472)
    sqb = sb("sqb", [128, 1024], BF16, 173568)
    t1 = sb("t1", [128, 1024], F32, 175616)
    meanbc = sb("meanbc", [128, 1024], F32, 179712)
    rstdbc = sb("rstdbc", [128, 1024], F32, 183808)
    xplstage = sb("xplstage", [128, 2, 1024], F32, 187904)
    dstage = sb("dstage", [128, 2, 2048], F32, 196096)
    dn16 = sb("dn16", [128, 8, 16, 128], BF16, 153088)
    hgrp = sb("hgrp", [128, 8, 1024], BF16, 212480)
    xpl2 = sb("xpl2", [128, 2, 1024], F32, 153088)
    rb16b = sb("rb16b", [128, 2, 1024], BF16, 161280)
    sqb2 = sb("sqb2", [128, 1024], BF16, 165376)
    t1b = sb("t1b", [128, 1024], F32, 167424)
    meanbc2 = sb("meanbc2", [128, 1024], F32, 171520)
    rstdbc2 = sb("rstdbc2", [128, 1024], F32, 175616)
    fstage = sb("fstage", [128, 2, 1024], F32, 179712)
    psA = nc.alloc_psum_tensor("psA", [128, 1024], F32)
    psB = nc.alloc_psum_tensor("psB", [128, 1024], F32)
    psO = nc.alloc_psum_tensor("psO", [128, 512], F32)
    psM = nc.alloc_psum_tensor("psM", [128, 512], F32)
    psC = nc.alloc_psum_tensor("psC", [128, 512], F32)
    psU = nc.alloc_psum_tensor("psU", [128, 512], F32)

    sems = {n: nc.alloc_semaphore(n) for n in
            ("ld", "st", "pe", "act", "dve", "pool", "cc",
             "x0", "x1", "r0", "r1", "xr0", "xr1", "us0", "us1",
             "dd0", "dd1", "p0", "p1", "c0", "c1", "xo0", "xo1",
             "yo0", "yo1", "rb0", "rb1", "c2", "c3")}

    s = Sched()
    wtok = {}
    rtok = {}

    def ps_half(which, half):
        t = psA if which == "A" else psB
        return t[:, half * 512:(half + 1) * 512]

    def mm(out_ap, lhsT, rhs, slot, start, stop, extra_waits=(),
           inc_anyway=False):
        if start:
            s.wait("tensor", rtok.get(slot))
        for t in extra_waits:
            s.wait("tensor", t)
        do_inc = stop or inc_anyway
        tok = s.op("tensor",
                   lambda E, o=out_ap, l=lhsT, r=rhs, st_=start, sp=stop:
                   E.matmul(o, l, r, start=st_, stop=sp),
                   inc="pe" if do_inc else None)
        if stop:
            wtok[slot] = tok
        return tok

    def read_slot(eng, slot, fn, inc, extra_waits=()):
        s.wait(eng, wtok.get(slot))
        for t in extra_waits:
            s.wait(eng, t)
        tok = s.op(eng, fn, inc=inc)
        rtok[slot] = tok
        return tok

    def dma_in(out_ap, in_ap, sem="ld", extra_waits=()):
        for t in extra_waits:
            s.wait("sync", t)
        return s.op("sync", lambda E, o=out_ap, i=in_ap: E.dma_start(out=o, in_=i),
                    inc=sem, n=16)

    def dma_out(out_ap, in_ap, sem="st", extra_waits=()):
        for t in extra_waits:
            s.wait("gpsimd", t)
        return s.op("gpsimd", lambda E, o=out_ap, i=in_ap: E.dma_start(out=o, in_=i),
                    inc=sem, n=16)

    dma_in(sm[:, :], smalls)
    dma_in(bv_sb[:, :], bv)
    t_sm = t_bv = ("ld", 32)

    s.op("gpsimd", lambda E: E.memset(ones_bf[:, :], 1.0), inc="pool")
    s.op("gpsimd", lambda E: E.memset(ones_f[:, :], 1.0), inc="pool")
    for v in range(4):
        s.op("gpsimd", lambda E, v=v: E.memset(masks[:, v, :], 1.0), inc="pool")
        s.wait("gpsimd", s.cur("pool"))
        s.op("gpsimd", lambda E, v=v: E.affine_select(
            out=masks[:, v, :], in_=masks[:, v, :], pattern=[[1, 512]],
            compare_op=ALU.is_ge, fill=0.0, base=-128 * v, channel_multiplier=-1),
            inc="pool")
    s.op("gpsimd", lambda E: E.memset(uw[0:1, 6:7], -0.5 * math.log(HD)),
         inc="pool")
    s.op("gpsimd", lambda E: E.memset(uw[0:1, 7:8], LN_EPS), inc="pool")
    t_const = s.cur("pool")
    s.wait("tensor", t_const)
    s.wait("vector", t_const)
    s.wait("scalar", t_const)

    s.wait("vector", t_bv)
    t_bvbf = s.op("vector", lambda E: E.tensor_copy(bv_bf[:, :], bv_sb[:, :]),
                  inc="dve")
    s.wait("vector", t_sm)
    t_dsos = s.op("vector", lambda E: E.tensor_scalar(
        dsos[:, :], sm[:, 88:104], sm[:, 104:105], None, ALU.mult), inc="dve")

    xstage_tok = [None, None]
    xblk_pe_tok = None

    wq_tok = [None] * 12
    wqs_tok = [None, None]

    def load_wq_piece(i):
        tl = dma_in(wst[i % 2][:, :], wq[i], sem=f"us{i % 2}",
                    extra_waits=[wqs_tok[i % 2]])
        s.wait("scalar", tl)
        tok = s.op("scalar",
                   lambda E, i=i: E.activation(
                       wq_bf[:, :, i * 128:(i + 1) * 128],
                       wst[i % 2][:, :].rearrange("p (dc f) -> p dc f", dc=16),
                       AF.Copy, bias=0.0, scale=1.0),
                   inc="act")
        wqs_tok[i % 2] = tok
        wq_tok[i] = tok

    load_wq_piece(0)

    qk_slots = [("A", 0), ("A", 1), ("B", 0), ("B", 1)]
    nslot = 0
    vslot = 0
    for tb in range(8):
        for sl in range(4):
            gslab = tb * 4 + sl
            tl = dma_in(xstage[:, sl % 2, :], xq[gslab], sem=f"x{sl % 2}",
                        extra_waits=[xstage_tok[sl % 2]])
            s.wait("vector", tl)
            if sl == 0:
                s.wait("vector", xblk_pe_tok)
            tok = s.op("vector",
                       lambda E, sl=sl: E.tensor_copy(
                           xblk[:, :, sl * 128:(sl + 1) * 128],
                           xstage[:, sl % 2, :].rearrange(
                               "p (dc f) -> p dc f", dc=16)),
                       inc="dve")
            xstage_tok[sl % 2] = tok
            if tb == 0:
                hi = 12 if sl == 3 else 2 + 3 * sl
                for i in range(1, hi):
                    if i < 12 and wq_tok[i] is None:
                        load_wq_piece(i)
        t_xblk = s.cur("dve")

        for et in range(8):
            slot = qk_slots[nslot % 4]
            nslot += 1
            for dc in range(16):
                mm(ps_half(*slot), wq_bf[:, dc, et * 128:(et + 1) * 128],
                   xblk[:, dc, :], slot, start=(dc == 0), stop=(dc == 15),
                   extra_waits=([wq_tok[et], t_xblk] if dc == 0 else []))
            xblk_pe_tok = wtok[slot]

            dst = Q_sb if et < 4 else K_sb
            h = et % 4
            col = tb * 512
            read_slot("scalar", slot,
                      lambda E, d=dst, h=h, c=col, sl_=slot, et=et:
                      E.activation(d[:, h, c:c + 512], ps_half(*sl_),
                                   AF.Identity, bias=sm[:, et:et + 1],
                                   scale=1.0),
                      inc="act", extra_waits=[t_sm])

        for tt in range(4):
            slot = ("O",) if vslot % 2 == 0 else ("M",)
            vslot += 1
            pst = psO if slot == ("O",) else psM
            for dc in range(16):
                mm(pst[:, :], xblk[:, dc, tt * 128:(tt + 1) * 128],
                   wq_bf[:, dc, 1024:1536], slot, start=(dc == 0), stop=False,
                   extra_waits=([t_xblk, wq_tok[8], wq_tok[9], wq_tok[10],
                                 wq_tok[11]] if dc == 0 else []))
            mm(pst[:, :], ones_bf[0:1, 0:128], bv_bf[0:1, :], slot,
               start=False, stop=True, extra_waits=[t_bvbf])
            xblk_pe_tok = wtok[slot]
            gt = tb * 4 + tt
            read_slot("scalar", slot,
                      lambda E, gt=gt, p=pst: E.activation(
                          V_sb[:, gt, :], p[:, :], AF.Copy, bias=0.0, scale=1.0),
                      inc="act")

    t_qkv_act = s.cur("act")
    rtok[("O", 0)] = rtok.get(("O",))
    rtok[("O", 1)] = rtok.get(("M",))
    rtok[("O", 2)] = None

    pt_tok = [None, None]
    c_read_tok = None
    pending_norm = []

    wout_tok = None
    t_qkv_pe = s.cur("pe")
    for dc in range(4):
        tl = dma_in(xstage[:, dc % 2, :], wo[dc], sem=f"x{dc % 2}",
                    extra_waits=[xstage_tok[dc % 2]])
        s.wait("vector", tl)
        s.wait("vector", t_qkv_pe)
        ct = s.op("vector",
                  lambda E, dc=dc: E.tensor_copy(wout_bf[:, dc, :],
                                                 xstage[:, dc % 2, :]),
                  inc="dve")
        xstage_tok[dc % 2] = ct
        wout_tok = ct

    rounds = [(h, side, ch) for h in range(4) for side in (0, 1)
              for ch in range(8)]
    rr_tok = [None, None, None]
    qs_tok = [None, None, None]
    for i, (h, side, ch) in enumerate(rounds):
        r3 = i % 3
        src_t = Q_sb if side == 0 else K_sb
        s.wait("vector", t_qkv_act)
        s.wait("vector", qs_tok[r3])
        sq_t = s.op("vector",
                    lambda E, src_t=src_t, h=h, ch=ch, r3=r3: E.tensor_mul(
                        qsqr[r3][:, :], src_t[:, h, ch * 512:(ch + 1) * 512],
                        src_t[:, h, ch * 512:(ch + 1) * 512]),
                    inc="dve")
        s.wait("tensor", rr_tok[r3])
        s.wait("tensor", sq_t)
        pe_t = s.op("tensor",
                    lambda E, r3=r3: E.matmul(
                        psC[32 * r3:32 * r3 + 1, :], ones_bf[:, 0:1],
                        qsqr[r3][:, :], start=True, stop=True), inc="pe")
        qs_tok[r3] = pe_t
        s.wait("vector", pe_t)
        rm_t = s.op("vector",
                    lambda E, r3=r3: E.reduce_max(
                        uw[0:1, r3:r3 + 1], psC[32 * r3:32 * r3 + 1, :],
                        axis=AX.X), inc="dve")
        rr_tok[r3] = rm_t
        s.wait("vector", rm_t)
        col = 8 + 2 * h + side
        if ch == 0:
            s.op("vector", lambda E, col=col, r3=r3: E.tensor_copy(
                uw[0:1, col:col + 1], uw[0:1, r3:r3 + 1]), inc="dve")
        else:
            s.op("vector", lambda E, col=col, r3=r3: E.tensor_max(
                uw[0:1, col:col + 1], uw[0:1, col:col + 1],
                uw[0:1, r3:r3 + 1]), inc="dve")
        s.wait("vector", s.cur("dve"))
    for h4 in range(4):
        s.op("vector", lambda E, h4=h4: E.tensor_mul(
            uw[0:1, h4:h4 + 1], uw[0:1, 8 + 2 * h4:9 + 2 * h4],
            uw[0:1, 9 + 2 * h4:10 + 2 * h4]), inc="dve")
        s.wait("vector", s.cur("dve"))
    t_prod4 = s.cur("dve")
    s.wait("scalar", t_prod4)
    t_ln4 = s.op("scalar", lambda E: E.activation(
        uw[0:1, 8:12], uw[0:1, 0:4], AF.Ln), inc="act")
    s.wait("scalar", t_ln4)
    t_u4 = s.op("scalar", lambda E: E.activation(
        uw[0:1, 12:16], uw[0:1, 8:12], AF.Exp, bias=uw[0:1, 6:7], scale=0.5),
        inc="act")
    s.wait("vector", t_u4)
    t_nub4 = s.op("vector", lambda E: E.tensor_scalar(
        nub[0:1, 0:4], uw[0:1, 12:16], -1.0, None, ALU.mult), inc="dve")
    for p_ in range(3):
        s.wait("tensor", rr_tok[p_])
    s.wait("tensor", t_nub4)
    t_bc4 = s.op("tensor", lambda E: E.matmul(
        psC[:, 0:4], ones_bf[0:1, :], nub[0:1, 0:4], start=True, stop=True),
        inc="pe")
    s.wait("scalar", t_bc4)
    t_negu4 = s.op("scalar", lambda E: E.activation(
        negu4[:, :], psC[:, 0:4], AF.Copy, bias=0.0, scale=1.0), inc="act")
    s.wait("scalar", t_negu4)
    for p_ in range(3):
        rtok[("M", p_)] = t_negu4

    def emit_norm(h, qg, t_rec, t_oev):
        p = qg % 2
        d1 = dma_out(recd.ap()[p:p + 1, :], recs[p][0:1, :], sem=f"rb{p}",
                     extra_waits=[t_rec])
        rtok[("RECF", p)] = d1
        bsrc = bass.AP(recd, p * 512, [[0, 128], [1, 512]])
        d2 = dma_out(recbc[:, p, :], bsrc, sem=f"rb{p}",
                     extra_waits=[rtok.get(("RB", p)), d1])
        s.wait("vector", d2)
        s.wait("vector", t_oev)
        t_onorm = s.op("vector",
                       lambda E, h=h, qg=qg: E.tensor_mul(
                           O_sb[:, h, qg * 512:(qg + 1) * 512],
                           O_sb[:, h, qg * 512:(qg + 1) * 512],
                           recbc[:, qg % 2, :]),
                       inc="dve")
        rtok[("RB", p)] = t_onorm
        return t_onorm

    def flush_norms(upto):
        while len(pending_norm) > upto:
            emit_norm(*pending_norm.pop(0))

    for h in range(4):
        for qg in range(8):
            npairs = 2 * qg + 2
            exp_toks = {}
            mask_toks = {}
            flush_norms(1)
            for pr in range(npairs + 1):
                if pr < npairs:
                    pst = psA if pr % 2 == 0 else psB
                    slot = ("S", pr % 2)
                    for half in range(2):
                        kt = 2 * pr + half
                        mm(pst[:, half * 512:(half + 1) * 512],
                           K_sb[:, h, kt * 128:(kt + 1) * 128],
                           Q_sb[:, h, qg * 512:(qg + 1) * 512],
                           slot, start=True, stop=True,
                           extra_waits=[t_qkv_act] if half == 0 else [])
                    read_slot("scalar", slot,
                              lambda E, pst=pst, pr=pr, h=h: E.activation(
                                  PT[:, pr % 2, :], pst[:, :], AF.Exp,
                                  bias=negu4[:, h:h + 1], scale=INV_SQRT_HD),
                              inc="act", extra_waits=[pt_tok[pr % 2]])
                    exp_toks[pr] = s.cur("act")
                    for half in range(2):
                        kt = 2 * pr + half
                        v = kt - 4 * qg
                        if 0 <= v <= 3:
                            s.wait("vector", exp_toks[pr])
                            mt = s.op("vector",
                                      lambda E, pr=pr, half=half, v=v:
                                      E.tensor_mul(
                                          PT[:, pr % 2,
                                             half * 512:(half + 1) * 512],
                                          PT[:, pr % 2,
                                             half * 512:(half + 1) * 512],
                                          masks[:, v, :]),
                                      inc="dve")
                            mask_toks[(pr, half)] = mt
                if pr > 0:
                    ppr = pr - 1
                    last = (ppr == npairs - 1)
                    po = [psO, psM, psU][qg % 3]
                    oslot = ("O", qg % 3)
                    mslot = ("M", qg % 3)
                    mrow = 32 * (qg % 3)
                    for half in range(2):
                        kt = 2 * ppr + half
                        ew = [exp_toks[ppr]]
                        if (ppr, half) in mask_toks:
                            ew.append(mask_toks[(ppr, half)])
                        mm(po[:, :], V_sb[:, kt, h * 128:(h + 1) * 128],
                           PT[:, ppr % 2, half * 512:(half + 1) * 512],
                           oslot, start=(kt == 0), stop=(last and half == 1),
                           extra_waits=ew)
                        mm(psC[mrow:mrow + 1, :], ones_bf[:, 0:1],
                           PT[:, ppr % 2, half * 512:(half + 1) * 512],
                           mslot, start=(kt == 0), stop=(last and half == 1),
                           extra_waits=[], inc_anyway=(half == 1))
                    pt_tok[ppr % 2] = s.cur("pe")
            po = [psO, psM, psU][qg % 3]
            oslot = ("O", qg % 3)
            mslot = ("M", qg % 3)
            mrow = 32 * (qg % 3)
            s.wait("vector", wtok[mslot])
            s.wait("vector", rtok.get(("RECF", qg % 2)))
            t_rec = s.op("vector", lambda E, qg=qg, mrow=mrow: E.reciprocal(
                recs[qg % 2][0:1, :], psC[mrow:mrow + 1, :]),
                inc="dve")
            rtok[mslot] = t_rec
            t_oev = read_slot("scalar", oslot,
                              lambda E, h=h, qg=qg, po=po: E.activation(
                                  O_sb[:, h, qg * 512:(qg + 1) * 512],
                                  po[:, :], AF.Copy, bias=0.0, scale=1.0),
                              inc="act")
            pending_norm.append((h, qg, t_rec, t_oev))

    flush_norms(0)
    t_attn_dve = s.cur("dve")


    cs_tok = [None] * 4
    op_slots = [("A", 0), ("A", 1), ("B", 0), ("B", 1)]
    ns = 0
    cseq = 0
    t_cc = {}
    for idx, tcg in enumerate([0, 2, 4, 6, 1, 3, 5, 7]):
        half = tcg % 2
        for et in range(16):
            slot = op_slots[ns % 4]
            for dc in range(4):
                mm(ps_half(*slot), wout_bf[:, dc, et * 128:(et + 1) * 128],
                   O_sb[:, dc, tcg * 512:(tcg + 1) * 512], slot,
                   start=(dc == 0), stop=(dc == 3),
                   extra_waits=([wout_tok, t_attn_dve] if dc == 0 else []))
            sl = cseq % 4
            ns += 1
            cseq += 1
            if et % 2 == 0:
                ev = read_slot("scalar", slot,
                               lambda E, sl=sl, s_=slot: E.activation(
                                   cstage[:, sl, :], ps_half(*s_), AF.Copy,
                                   bias=0.0, scale=1.0),
                               inc="act", extra_waits=[cs_tok[sl]])
            else:
                ev = read_slot("vector", slot,
                               lambda E, sl=sl, s_=slot: E.tensor_copy(
                                   cstage[:, sl, :], ps_half(*s_)),
                               inc="dve", extra_waits=[cs_tok[sl]])
            r0 = (tcg // 2) * 2048 + et * 128
            dst = cbuf2[half].ap()[r0:r0 + 128, :]
            if sl % 2 == 0:
                cs_tok[sl] = dma_out(dst, cstage[:, sl, :], sem=f"c{sl}",
                                     extra_waits=[ev])
            else:
                cs_tok[sl] = dma_in(dst, cstage[:, sl, :], sem=f"c{sl}",
                                    extra_waits=[ev])
        if idx == 3 or idx == 7:
            h_ = 0 if idx == 3 else 1
            for cn in ("c0", "c1", "c2", "c3"):
                s.wait("gpsimd", s.cur(cn))
            t_cc[h_] = s.op("gpsimd", lambda E, h_=h_: E.collective_compute(
                "ReduceScatter", ALU.add,
                replica_groups=[[0, 1, 2, 3], [4, 5, 6, 7]],
                ins=[cbuf2[h_].ap().opt()], outs=[rbuf2[h_].ap().opt()]),
                inc="cc")
        if idx == 3:
            t_op_pe_evens = s.cur("pe")
    t_op_pe = s.cur("pe")

    up_pref = []
    for i in range(2):
        up_pref.append(dma_in(ustage[:, i, :], up[i], sem=f"us{i}",
                              extra_waits=[t_op_pe]))
    s.wait("gpsimd", t_cc[1])

    def layer_norm(src_big, rb, sq, tt1, mbc, rbc, out_cb, start_tok=None,
                   src_toks=None):
        slots_ps = [psO, psM, psC, psU]
        rb_tok = [start_tok, start_tok]
        for et in range(16):
            sl = et % 2
            s.wait("vector", rb_tok[sl])
            s.wait("vector", rb_tok[1 - sl])
            s.wait("scalar", rb_tok[sl])
            s.wait("scalar", rb_tok[1 - sl])
            if src_toks is not None:
                s.wait("scalar", src_toks[et])
            tcast = s.op("scalar",
                         lambda E, et=et, sl=sl, rb=rb, src=src_big:
                         E.activation(rb[:, sl, :], src[:, et, :], AF.Copy,
                                      bias=0.0, scale=1.0), inc="act")
            s.wait("vector", tcast)
            tsq = s.op("vector",
                       lambda E, sl=sl, rb=rb, sq=sq: E.tensor_mul(
                           sq[:, :], rb[:, sl, :], rb[:, sl, :]), inc="dve")
            for half in range(2):
                mm(slots_ps[half][0:1, :], ones_bf[:, 0:1],
                   rb[:, sl, half * 512:(half + 1) * 512], ("LNM", half),
                   start=(et == 0), stop=(et == 15), extra_waits=[tcast],
                   inc_anyway=True)
            for half in range(2):
                mm(slots_ps[2 + half][0:1, :], ones_bf[:, 0:1],
                   sq[:, half * 512:(half + 1) * 512], ("LNV", half),
                   start=(et == 0), stop=(et == 15), extra_waits=[tsq],
                   inc_anyway=True)
            rb_tok[sl] = s.cur("pe")
        t_pe_rows = s.cur("pe")
        s.wait("vector", t_pe_rows)
        for half in range(2):
            s.op("vector", lambda E, half=half: E.tensor_scalar(
                rows[0:1, half * 512:half * 512 + 512],
                slots_ps[half][0:1, :], 1.0 / D, None, ALU.mult), inc="dve")
            s.op("vector", lambda E, half=half: E.tensor_scalar(
                rows[0:1, 1024 + half * 512:1024 + half * 512 + 512],
                slots_ps[2 + half][0:1, :], 1.0 / D, None, ALU.mult), inc="dve")
        s.wait("vector", s.cur("dve"))
        s.op("vector", lambda E: E.tensor_mul(
            rows[0:1, 2048:3072], rows[0:1, 0:1024], rows[0:1, 0:1024]),
            inc="dve")
        s.wait("vector", s.cur("dve"))
        t_var = s.op("vector", lambda E: E.tensor_sub(
            rows[0:1, 1024:2048], rows[0:1, 1024:2048], rows[0:1, 2048:3072]),
            inc="dve")
        s.wait("vector", t_var)
        t_var = s.op("vector", lambda E: E.tensor_scalar(
            rows[0:1, 1024:2048], rows[0:1, 1024:2048], LN_EPS, None, ALU.add),
            inc="dve")
        for key in (("LNM", 0), ("LNM", 1), ("LNV", 0), ("LNV", 1)):
            rtok[key] = t_var
        s.wait("scalar", t_var)
        t_lnv = s.op("scalar", lambda E: E.activation(
            rows[0:1, 2048:3072], rows[0:1, 1024:2048], AF.Ln), inc="act")
        s.wait("scalar", t_lnv)
        t_rstd = s.op("scalar", lambda E: E.activation(
            rows[0:1, 3072:4096], rows[0:1, 2048:3072], AF.Exp, scale=-0.5),
            inc="act")
        s.wait("vector", t_rstd)
        s.op("vector", lambda E: E.tensor_mul(
            rows[0:1, 2048:3072], rows[0:1, 3072:4096], rows[0:1, 3072:4096]),
            inc="dve")
        s.wait("vector", s.cur("dve"))
        s.op("vector", lambda E: E.tensor_mul(
            rows[0:1, 2048:3072], rows[0:1, 2048:3072], rows[0:1, 1024:2048]),
            inc="dve")
        s.wait("vector", s.cur("dve"))
        s.op("vector", lambda E: E.tensor_scalar(
            rows[0:1, 2048:3072], rows[0:1, 2048:3072], -0.5, 1.5,
            ALU.mult, ALU.add), inc="dve")
        s.wait("vector", s.cur("dve"))
        t_rstd = s.op("vector", lambda E: E.tensor_mul(
            rows[0:1, 3072:4096], rows[0:1, 3072:4096], rows[0:1, 2048:3072]),
            inc="dve")
        t_bc_done = None
        for half in range(2):
            for ri, dst in ((0, mbc), (3, rbc)):
                src_row = rows[0:1,
                               ri * 1024 + half * 512:ri * 1024 + half * 512 + 512]
                s.wait("tensor", t_rstd if ri == 3 else t_var)
                s.wait("tensor", rtok.get(("LNBC",)))
                tbc = s.op("tensor",
                           lambda E, sr=src_row: E.matmul(
                               psC[:, :], ones_f[0:1, :], sr, start=True,
                               stop=True), inc="pe")
                s.wait("scalar", tbc)
                tcp = s.op("scalar",
                           lambda E, dst=dst, half=half: E.activation(
                               dst[:, half * 512:(half + 1) * 512], psC[:, :],
                               AF.Copy, bias=0.0, scale=1.0), inc="act")
                rtok[("LNBC",)] = tcp
                t_bc_done = tcp
        for et in range(16):
            s.wait("vector", t_bc_done)
            tsub = s.op("vector", lambda E, et=et, tt1=tt1, src=src_big, mbc=mbc:
                        E.tensor_sub(tt1[:, :], src[:, et, :], mbc[:, :]),
                        inc="dve")
            s.wait("vector", tsub)
            out_cb(et)
            s.wait("vector", s.cur("dve"))
        return t_bc_done

    xpl_tok = [None, None]
    xln_done = [None] * 2

    r_tok = [None, None]
    xr_tok = [None, None]
    rb_tok = [None, None]

    def ln_half(half, t_ready_tok):
        c0 = half * 512
        for et in range(16):
            sl = et % 2
            ew = [r_tok[sl]]
            if et < 2:
                ew.append(t_ready_tok)
            tl1 = dma_in(rstage[:, sl, 0:512],
                         rbuf2[half].ap()[et * 128:(et + 1) * 128, :],
                         sem=f"r{sl}", extra_waits=ew)
            tl2 = dma_in(xrstage[:, sl, :], xres[et][:, c0:c0 + 512],
                         sem=f"xr{sl}", extra_waits=[xr_tok[sl]])
            s.wait("vector", tl1)
            s.wait("vector", tl2)
            s.wait("vector", t_sm)
            tres = s.op("vector",
                        lambda E, et=et, sl=sl, c0=c0: E.scalar_tensor_tensor(
                            bigf[:, et, c0:c0 + 512], rstage[:, sl, 0:512],
                            sm[:, 8 + et:9 + et], xrstage[:, sl, :],
                            ALU.add, ALU.add),
                        inc="dve")
            r_tok[sl] = tres
            xr_tok[sl] = tres
            s.wait("scalar", tres)
            s.wait("scalar", rb_tok[sl])
            s.wait("scalar", rb_tok[1 - sl])
            tcast = s.op("scalar",
                         lambda E, et=et, sl=sl, c0=c0: E.activation(
                             rb16[:, sl, 0:512], bigf[:, et, c0:c0 + 512],
                             AF.Copy, bias=0.0, scale=1.0), inc="act")
            s.wait("vector", tcast)
            tsq = s.op("vector",
                       lambda E, sl=sl: E.tensor_mul(
                           sqb[:, 0:512], rb16[:, sl, 0:512],
                           rb16[:, sl, 0:512]), inc="dve")
            mm(psO[0:1, :], ones_bf[:, 0:1], rb16[:, sl, 0:512],
               ("LNM", half), start=(et == 0), stop=(et == 15),
               extra_waits=[tcast], inc_anyway=True)
            mm(psM[0:1, :], ones_bf[:, 0:1], sqb[:, 0:512],
               ("LNV", half), start=(et == 0), stop=(et == 15),
               extra_waits=[tsq], inc_anyway=True)
            rb_tok[sl] = s.cur("pe")
        t_pe_rows = s.cur("pe")
        s.wait("vector", t_pe_rows)
        R = rows[0:1, :]
        mrow = rows[0:1, c0:c0 + 512]
        vrow = rows[0:1, 1024 + c0:1024 + c0 + 512]
        srow = rows[0:1, 2048 + c0:2048 + c0 + 512]
        rrow = rows[0:1, 3072 + c0:3072 + c0 + 512]
        s.op("vector", lambda E: E.tensor_scalar(
            mrow, psO[0:1, :], 1.0 / D, None, ALU.mult), inc="dve")
        s.op("vector", lambda E: E.tensor_scalar(
            vrow, psM[0:1, :], 1.0 / D, None, ALU.mult), inc="dve")
        s.wait("vector", s.cur("dve"))
        s.op("vector", lambda E: E.tensor_mul(srow, mrow, mrow), inc="dve")
        s.wait("vector", s.cur("dve"))
        s.op("vector", lambda E: E.tensor_sub(vrow, vrow, srow), inc="dve")
        s.wait("vector", s.cur("dve"))
        t_var = s.op("vector", lambda E: E.tensor_scalar(
            vrow, vrow, LN_EPS, None, ALU.add), inc="dve")
        rtok[("LNM", half)] = t_var
        rtok[("LNV", half)] = t_var
        s.wait("scalar", t_var)
        t_lnv = s.op("scalar", lambda E: E.activation(srow, vrow, AF.Ln),
                     inc="act")
        s.wait("scalar", t_lnv)
        t_rstd = s.op("scalar", lambda E: E.activation(
            rrow, srow, AF.Exp, scale=-0.5), inc="act")
        s.wait("vector", t_rstd)
        s.op("vector", lambda E: E.tensor_mul(srow, rrow, rrow), inc="dve")
        s.wait("vector", s.cur("dve"))
        s.op("vector", lambda E: E.tensor_mul(srow, srow, vrow), inc="dve")
        s.wait("vector", s.cur("dve"))
        s.op("vector", lambda E: E.tensor_scalar(
            srow, srow, -0.5, 1.5, ALU.mult, ALU.add), inc="dve")
        s.wait("vector", s.cur("dve"))
        t_rstd = s.op("vector", lambda E: E.tensor_mul(rrow, rrow, srow),
                      inc="dve")
        s.wait("tensor", t_var)
        s.wait("tensor", rtok.get(("LNBC",)))
        tb1 = s.op("tensor", lambda E: E.matmul(
            psC[:, :], ones_f[0:1, :], mrow, start=True, stop=True), inc="pe")
        s.wait("scalar", tb1)
        cm1 = s.op("scalar", lambda E: E.activation(
            meanbc[:, 0:512], psC[:, :], AF.Copy, bias=0.0, scale=1.0),
            inc="act")
        s.wait("tensor", t_rstd)
        s.wait("tensor", rtok.get(("LNBC2",)))
        tb2 = s.op("tensor", lambda E: E.matmul(
            psU[:, :], ones_f[0:1, :], rrow, start=True, stop=True), inc="pe")
        s.wait("scalar", tb2)
        cm2 = s.op("scalar", lambda E: E.activation(
            rstdbc[:, 0:512], psU[:, :], AF.Copy, bias=0.0, scale=1.0),
            inc="act")
        rtok[("LNBC",)] = cm1
        rtok[("LNBC2",)] = cm2
        for et in range(16):
            sl = et % 2
            s.wait("vector", cm1)
            s.wait("vector", cm2)
            tsub = s.op("vector",
                        lambda E, et=et, c0=c0: E.tensor_sub(
                            t1[:, 0:512], bigf[:, et, c0:c0 + 512],
                            meanbc[:, 0:512]), inc="dve")
            s.wait("vector", tsub)
            tk1 = s.op("vector", lambda E, et=et, c0=c0: E.tensor_mul(
                xln[:, et, c0:c0 + 512], t1[:, 0:512], rstdbc[:, 0:512]),
                inc="dve")
            s.wait("vector", xpl_tok[sl])
            tk2 = s.op("vector", lambda E, sl=sl, c0=c0: E.tensor_mul(
                xplstage[:, sl, 0:512], t1[:, 0:512], rstdbc[:, 0:512]),
                inc="dve")
            s.wait("vector", s.cur("dve"))
            xpl_tok[sl] = dma_out(xln1d[et][:, c0:c0 + 512],
                                  xplstage[:, sl, 0:512], sem=f"xo{sl}",
                                  extra_waits=[tk2])
        xln_done[half] = s.cur("dve")

    ln_half(0, t_cc[0])
    ln_half(1, t_cc[1])
    t_ln1 = s.cur("dve")
    t_xpl_w0 = s.cur("xo0")
    t_xpl_w1 = s.cur("xo1")

    us_tok = [None, None]
    ub_read = [None, None]
    sign_tok = [None, None]
    ds_tok = [None, None]
    dn_read = [None] * 8
    hg_read = [None] * 8
    acc_tok = [[None, None] for _ in range(16)]
    hd_slots = [("A", 0), ("A", 1)]
    dd_slots = [("B", 0), ("B", 1)]

    for g in range(8):
        for hci in range(8):
            ht = g * 8 + hci
            usl = ht % 2
            if ht < 2:
                tl = up_pref[ht]
            else:
                tl = dma_in(ustage[:, usl, :], up[ht], sem=f"us{usl}",
                            extra_waits=[sign_tok[usl]])
            s.wait("scalar", tl)
            s.wait("scalar", ub_read[usl])
            tsg = s.op("scalar",
                       lambda E, usl=usl: E.activation(
                           ub16[:, usl, :], ustage[:, usl, :], AF.Sign),
                       inc="act")
            sign_tok[usl] = tsg
            for tc in range(2):
                for dc in range(16):
                    ew = []
                    if dc == 0:
                        ew = [tsg, xln_done[tc]]
                    mm(ps_half(*hd_slots[tc]),
                       ub16[:, usl, dc * 128:(dc + 1) * 128],
                       xln[:, dc, tc * 512:(tc + 1) * 512], hd_slots[tc],
                       start=(dc == 0), stop=(dc == 15), extra_waits=ew)
            ub_read[usl] = wtok[hd_slots[1]]
            for tc in range(2):
                read_slot("vector", hd_slots[tc],
                          lambda E, hci=hci, tc=tc, ht=ht, s_=hd_slots[tc]:
                          E.tensor_scalar(
                              hgrp[:, hci, tc * 512:(tc + 1) * 512],
                              ps_half(*s_), sm[:, 24 + ht:25 + ht], 0.0,
                              ALU.mult, ALU.max),
                          inc="dve", extra_waits=[hg_read[hci], t_sm])
        t_relu_g = s.cur("dve")

        for hci in range(8):
            dsl = hci % 2
            ew = [ds_tok[dsl]]
            if g == 0 and hci < 2:
                ew += [t_ln1, t_xpl_w0, t_xpl_w1]
            tl = dma_in(dstage[:, dsl, :], down[g, hci], sem=f"dd{dsl}",
                        extra_waits=ew)
            s.wait("scalar", tl)
            s.wait("scalar", dn_read[hci])
            if g == 0 and hci < 2:
                s.wait("scalar", t_xpl_w0)
                s.wait("scalar", t_xpl_w1)
                s.wait("scalar", t_ln1)
            tsg = s.op("scalar",
                       lambda E, hci=hci, dsl=dsl: E.activation(
                           dn16[:, hci, :, :].rearrange("p a b -> p (a b)"),
                           dstage[:, dsl, :], AF.Sign),
                       inc="act")
            ds_tok[dsl] = tsg
        t_sign_g = s.cur("act")

        for et in range(16):
            for tc in range(2):
                for hci in range(8):
                    ew = []
                    if hci == 0:
                        ew = [t_sign_g, t_relu_g]
                    mm(ps_half(*dd_slots[tc]), dn16[:, hci, et, :],
                       hgrp[:, hci, tc * 512:(tc + 1) * 512], dd_slots[tc],
                       start=(hci == 0), stop=(hci == 7), extra_waits=ew)
            for tc in range(2):
                if g == 0:
                    fn = (lambda E, et=et, tc=tc, s_=dd_slots[tc]:
                          E.tensor_copy(bigf[:, et, tc * 512:(tc + 1) * 512],
                                        ps_half(*s_)))
                else:
                    fn = (lambda E, et=et, tc=tc, s_=dd_slots[tc]:
                          E.tensor_add(bigf[:, et, tc * 512:(tc + 1) * 512],
                                       bigf[:, et, tc * 512:(tc + 1) * 512],
                                       ps_half(*s_)))
                read_slot("vector", dd_slots[tc], fn, inc="dve")
                acc_tok[et][tc] = s.cur("dve")
        t_down_g = s.cur("pe")
        for hci in range(8):
            hg_read[hci] = t_down_g
            dn_read[hci] = t_down_g
    t_down_pe = s.cur("pe")
    t_sign_all = s.cur("act")

    x2_tok = [None, None]
    res2_tok = [None] * 16
    for et in range(16):
        sl = et % 2
        ew = [x2_tok[sl]]
        if et < 2:
            ew += [t_down_pe, t_sign_all, t_xpl_w0, t_xpl_w1]
        tl = dma_in(xpl2[:, sl, :], xln1d[et], sem=f"p{sl}", extra_waits=ew)
        s.wait("vector", tl)
        s.wait("vector", t_dsos)
        tres = s.op("vector",
                    lambda E, et=et, sl=sl: E.scalar_tensor_tensor(
                        bigf[:, et, :], bigf[:, et, :], dsos[:, et:et + 1],
                        xpl2[:, sl, :], ALU.mult, ALU.add),
                    inc="dve")
        x2_tok[sl] = tres
        res2_tok[et] = tres

    f_tok = [None, None]
    y_writes = []

    def ln2_out(et):
        sl = et % 2
        s.wait("vector", f_tok[sl])
        tk = s.op("vector", lambda E, sl=sl: E.tensor_mul(
            fstage[:, sl, :], t1b[:, :], rstdbc2[:, :]), inc="dve")
        dt = dma_out(y[et], fstage[:, sl, :], sem=f"yo{sl}",
                     extra_waits=[tk])
        f_tok[sl] = dt
        y_writes.append(dt)

    layer_norm(bigf, rb16b, sqb2, t1b, meanbc2, rstdbc2, ln2_out,
               start_tok=t_down_pe, src_toks=res2_tok)

    s.wait("gpsimd", s.cur("yo0"))
    s.wait("gpsimd", s.cur("yo1"))

    with nc.Block() as block:
        def runner(eng_name):
            def go(E):
                for item in s.ops[eng_name]:
                    if item[0] == "w":
                        _, sname, v = item
                        E.wait_ge(sems[sname], v)
                    else:
                        _, fn, inc, n = item
                        ins = fn(E)
                        if inc is not None:
                            ins.then_inc(sems[inc], n)
            return go

        block.sync(runner("sync"))
        block.gpsimd(runner("gpsimd"))
        block.vector(runner("vector"))
        block.scalar(runner("scalar"))
        block.tensor(runner("tensor"))

    return nc


def _prep_inputs(L0_attn_x, L0_attn_w_qkv, L0_attn_b_qkv, L0_attn_w_out,
                 L0_attn_b_out, L0_ffn_up_weight, L0_ffn_down_weight,
                 L0_ffn_up_scale, L0_ffn_down_scale, L0_ffn_output_scale):
    x = np.asarray(L0_attn_x, np.float32)
    wqkv = np.asarray(L0_attn_w_qkv, np.float32)
    bqkv = np.asarray(L0_attn_b_qkv, np.float32)
    wout = np.asarray(L0_attn_w_out, np.float32)
    bout = np.asarray(L0_attn_b_out, np.float32)
    upw = np.asarray(L0_ffn_up_weight, np.float32)
    dnw = np.asarray(L0_ffn_down_weight, np.float32)
    usc = np.asarray(L0_ffn_up_scale, np.float32)
    dsc = np.asarray(L0_ffn_down_scale, np.float32)
    osc = np.asarray(L0_ffn_output_scale, np.float32).reshape(1)

    up_p = np.ascontiguousarray(
        upw.reshape(64, 128, 16, 128).transpose(0, 3, 2, 1)).reshape(
        64, 128, 2048)
    down_p = np.ascontiguousarray(dnw.T).reshape(8, 8, 128, 2048)
    woutT = np.ascontiguousarray(wout.T)

    in_maps = []
    xq_cache = {}
    xt_cache = {}
    for c in range(8):
        b, j = c // 4, c % 4
        if b not in xq_cache:
            xq_cache[b] = np.ascontiguousarray(
                x[b].reshape(32, 128, 16, 128).transpose(0, 3, 2, 1)
            ).reshape(32, 128, 2048)
            xt_cache[b] = np.ascontiguousarray(x[b].T)
        rows = []
        for sec in range(3):
            base = sec * 2048 + j * 512
            rows.append(np.arange(base, base + 512))
        rows_idx = np.concatenate(rows)
        wsl = wqkv[rows_idx]
        wq_p = np.ascontiguousarray(
            wsl.reshape(12, 128, 16, 128).transpose(0, 3, 2, 1)).reshape(
            12, 128, 2048)
        bqk = bqkv[rows_idx[:1024]].reshape(8, 128).T
        bv_r = np.ascontiguousarray(bqkv[rows_idx[1024:]].reshape(1, 512))
        wout_p = np.ascontiguousarray(woutT[j * 512:(j + 1) * 512]).reshape(
            4, 128, 2048)
        tok0 = j * 1024
        xres_p = np.ascontiguousarray(
            xt_cache[b][:, tok0:tok0 + 1024]).reshape(16, 128, 1024)
        sm_a = np.zeros((128, 112), np.float32)
        sm_a[:, 0:8] = bqk
        sm_a[:, 8:24] = bout.reshape(16, 128).T
        sm_a[:, 24:88] = usc.reshape(64, 128).T
        sm_a[:, 88:104] = dsc.reshape(16, 128).T
        sm_a[:, 104] = osc[0]
        in_maps.append({
            "xq": xq_cache[b], "wq": wq_p, "wo": wout_p, "xres": xres_p,
            "up": up_p, "down": down_p, "smalls": sm_a, "bv": bv_r,
        })
    return in_maps


_NC_CACHE = {}


def kernel(**inputs):
    in_maps = _prep_inputs(**inputs)
    if "nc" not in _NC_CACHE:
        _NC_CACHE["nc"] = build_program()
    nc = _NC_CACHE["nc"]
    res = run_bass_kernel_spmd(nc, in_maps, core_ids=list(range(8)))
    out = np.empty((2, 4096, 2048), np.float32)
    for c in range(8):
        b, j = c // 4, c % 4
        yc = np.asarray(res.results[c]["y"])
        out[b, j * 1024:(j + 1) * 1024, :] = yc.transpose(2, 0, 1).reshape(
            1024, 2048)
    return out
